# version 02
# baseline (speedup 1.0000x reference)
import numpy as np

import concourse.bass as bass
import concourse.mybir as mybir
from concourse.bass_utils import run_bass_kernel_spmd

F32 = mybir.dt.float32
BF16 = mybir.dt.bfloat16
AL = mybir.AluOpType
AF = mybir.ActivationFunctionType
AX = mybir.AxisListType.X

T = 2048
D = 1024
H = 16
E = 16
II = 512
IS = 2048
NT = 16
NSEQ = 2
FMAXP = 3.0e38

ARENA_N = 50176

ENGS = ("pe", "ve", "ae", "gp", "sy")


class Sem:
    def __init__(self, name):
        self.name = name
        self.val = 0
        self.h = None


class Buf:
    __slots__ = ("w", "r")

    def __init__(self, reuse=()):
        self.w = None
        self.r = []
        for ob in reuse:
            if ob.w is not None:
                self.r.append(ob.w)
            self.r.extend(ob.r)


class KB:
    def __init__(self):
        self.prog = {e: [] for e in ENGS}
        self.wm = {e: {} for e in ENGS}
        self.esem = {e: Sem(e) for e in ENGS}
        self.dsem = {"gp": Sem("dgp"), "sy": Sem("dsy"), "ae": Sem("dae")}
        self.bar = Sem("bar")

    def all_sems(self):
        return list(self.esem.values()) + list(self.dsem.values()) + [self.bar]

    def op(self, eng, fn, reads=(), writes=(), extra=(), dma=False):
        deps = []
        for b in reads:
            if b.w is not None:
                deps.append(b.w)
        for b in writes:
            if b.w is not None:
                deps.append(b.w)
            deps.extend(b.r)
        deps.extend(extra)
        best = {}
        for s, v in deps:
            if best.get(s.name, (None, -1))[1] < v:
                best[s.name] = (s, v)
        waits = []
        wm = self.wm[eng]
        for s, v in best.values():
            if wm.get(s.name, 0) < v:
                wm[s.name] = v
                waits.append((s, v))
        sem = self.dsem[eng] if dma else self.esem[eng]
        n = 16 if dma else 1
        sem.val += n
        tok = (sem, sem.val)
        self.prog[eng].append((waits, fn, sem, n))
        for b in reads:
            b.r.append(tok)
        for b in writes:
            b.w = tok
            b.r = []
        return tok

    def barrier(self):
        for e, ds in self.dsem.items():
            if ds.val > 0:
                self.prog[e].append(([(ds, ds.val)], None, None, 0))
        target = self.bar.val + len(ENGS)
        for e in ENGS:
            self.prog[e].append(([], "BARINC", self.bar, 1))
            self.prog[e].append(([(self.bar, target)], None, None, 0))
        self.bar.val = target
        snap = {}
        for s in self.all_sems():
            snap[s.name] = s.val
        for e in ENGS:
            self.wm[e] = dict(snap)

    def run(self, engines):
        for ename, eng in engines.items():
            for waits, fn, sem, n in self.prog[ename]:
                for s, v in waits:
                    eng.wait_ge(s.h, v)
                if fn is None:
                    continue
                if fn == "BARINC":
                    eng.nop().then_inc(sem.h, n)
                    continue
                ins = fn(eng)
                if sem is not None:
                    ins.then_inc(sem.h, n)


def build_program():
    nc = bass.Bass("TRN2", target_bir_lowering=False)

    xin = nc.dram_tensor("x", [NSEQ * T, D], F32, kind="ExternalInput").ap()
    wqkv = nc.dram_tensor("wqkv", [D, 3 * D], F32, kind="ExternalInput").ap()
    bqkv = nc.dram_tensor("bqkv", [3 * D], F32, kind="ExternalInput").ap()
    w1p = nc.dram_tensor("w1p", [D, 64], F32, kind="ExternalInput").ap()
    w2p = nc.dram_tensor("w2p", [D, 64], F32, kind="ExternalInput").ap()
    pscale = nc.dram_tensor("pscale", [H], F32, kind="ExternalInput").ap()
    woutw = nc.dram_tensor("wout", [D, D], F32, kind="ExternalInput").ap()
    boutw = nc.dram_tensor("bout", [D], F32, kind="ExternalInput").ap()
    gate = nc.dram_tensor("gate", [E, D], F32, kind="ExternalInput").ap()
    eup = nc.dram_tensor("eup", [E, D, II], F32, kind="ExternalInput").ap()
    edn = nc.dram_tensor("edn", [E, II, D], F32, kind="ExternalInput").ap()
    shupw = nc.dram_tensor("shup", [D, IS], F32, kind="ExternalInput").ap()
    shdnw = nc.dram_tensor("shdn", [IS, D], F32, kind="ExternalInput").ap()
    yout = nc.dram_tensor("y", [NSEQ * T, D], F32, kind="ExternalOutput").ap()
    h1sp = nc.dram_tensor("h1sp", [T, D], F32, kind="Internal").ap()

    K = KB()

    from contextlib import ExitStack
    es = ExitStack()
    with es:
        AR_h = es.enter_context(nc.sbuf_tensor("AR", [128, ARENA_N], F32))
        IDB_h = es.enter_context(nc.sbuf_tensor("IDB", [128, 128], BF16))
        ID32_h = es.enter_context(nc.sbuf_tensor("ID32", [128, 128], F32))
        IOTA_h = es.enter_context(nc.sbuf_tensor("IOTA", [128, 128], F32))
        ME_h = es.enter_context(nc.sbuf_tensor("ME", [128, 16], F32))
        ME2_h = es.enter_context(nc.sbuf_tensor("ME2", [128, 16], F32))
        ONESB_h = es.enter_context(nc.sbuf_tensor("ONESB", [1, 128], BF16))
        ONES32_h = es.enter_context(nc.sbuf_tensor("ONES32", [1, 128], F32))
        BQKS_h = es.enter_context(nc.sbuf_tensor("BQKS", [128, 16], F32))
        PSCROW_h = es.enter_context(nc.sbuf_tensor("PSCROW", [1, 16], F32))
        PSCB_h = es.enter_context(nc.sbuf_tensor("PSCB", [128, 16], F32))
        PSC6_h = es.enter_context(nc.sbuf_tensor("PSC6", [128, 96], F32))
        GATET_h = es.enter_context(nc.sbuf_tensor("GATET", [128, 128], F32))
        BVBO_h = es.enter_context(nc.sbuf_tensor("BVBO", [1, 2048], BF16))
        NRH_h = es.enter_context(nc.sbuf_tensor("NRH", [128, 8], F32))
        RSH_h = es.enter_context(nc.sbuf_tensor("RSH", [128, 8], F32))
        EPI_h = es.enter_context(nc.sbuf_tensor("EPI", [128, 24], F32))
        MV_h = es.enter_context(nc.sbuf_tensor("MV", [128, 8], F32))
        ST6_h = es.enter_context(nc.sbuf_tensor("ST6", [128, 12], F32))
        RT1_h = es.enter_context(nc.sbuf_tensor("RT1", [128, 4], F32))
        RSC_h = es.enter_context(nc.sbuf_tensor("RSC", [128, 80], F32))
        EPSB_h = es.enter_context(nc.sbuf_tensor("EPSB", [128, 1], F32))
        TRI_h = es.enter_context(nc.sbuf_tensor("TRI", [128, 128], F32))
        PS_h = es.enter_context(nc.psum_tensor("PS", [128, 4096], F32))
        h_pe = es.enter_context(nc.semaphore("s_pe"))
        h_ve = es.enter_context(nc.semaphore("s_ve"))
        h_ae = es.enter_context(nc.semaphore("s_ae"))
        h_gp = es.enter_context(nc.semaphore("s_gp"))
        h_sy = es.enter_context(nc.semaphore("s_sy"))
        h_dgp = es.enter_context(nc.semaphore("s_dgp"))
        h_dsy = es.enter_context(nc.semaphore("s_dsy"))
        h_dae = es.enter_context(nc.semaphore("s_dae"))
        h_bar = es.enter_context(nc.semaphore("s_bar"))
        K.esem["pe"].h = h_pe
        K.esem["ve"].h = h_ve
        K.esem["ae"].h = h_ae
        K.esem["gp"].h = h_gp
        K.esem["sy"].h = h_sy
        K.dsem["gp"].h = h_dgp
        K.dsem["sy"].h = h_dsy
        K.dsem["ae"].h = h_dae
        K.bar.h = h_bar

        AR = AR_h.ap()
        IDB = IDB_h.ap()
        ID32 = ID32_h.ap()
        IOTA = IOTA_h.ap()
        ME = ME_h.ap()
        ME2 = ME2_h.ap()
        ONESB = ONESB_h.ap()
        ONES32 = ONES32_h.ap()
        BQKS = BQKS_h.ap()
        PSCROW = PSCROW_h.ap()
        PSCB = PSCB_h.ap()
        PSC6 = PSC6_h.ap()
        GATET = GATET_h.ap()
        BVBO = BVBO_h.ap()
        NRH = NRH_h.ap()
        RSH = RSH_h.ap()
        EPI = EPI_h.ap()
        MV = MV_h.ap()
        ST6 = ST6_h.ap()
        RT1 = RT1_h.ap()
        RSC = RSC_h.ap()
        EPSB = EPSB_h.ap()
        bEPSB = Buf()
        TRI = TRI_h.ap()
        bTRI = Buf()
        PS = PS_h.ap()

        def a32(off, n):
            return AR[:, off:off + n]

        def a16(off, n32):
            return AR[:, off:off + n32].bitcast(BF16)

        def bk(i, n=1):
            return PS[:, i * 512:(i + n) * 512]

        def bkb(i, n=1):
            return PS[:, i * 512:(i + n) * 512].bitcast(BF16)

        op = K.op

        bIDB = Buf(); bID32 = Buf(); bIOTA = Buf(); bME = Buf(); bME2 = Buf()
        bONESB = Buf(); bONES32 = Buf(); bBQKS = Buf(); bPSCROW = Buf()
        bPSCB = Buf(); bPSC6 = Buf(); bGATET = Buf(); bBVBO = Buf()
        bNRH = Buf(); bRSH = Buf(); bEPI = Buf(); bMVb = Buf(); bST6 = Buf()
        bRT1 = Buf(); bRSC = Buf()
        bH1SP = Buf()
        psb = [Buf() for _ in range(8)]

        op("gp", lambda g: g.iota(IOTA, pattern=[[1, 128]], base=0,
                                  channel_multiplier=-1,
                                  allow_small_or_imprecise_dtypes=True),
           writes=[bIOTA])
        op("gp", lambda g: g.iota(ME, pattern=[[128, 16]], base=1,
                                  channel_multiplier=1,
                                  allow_small_or_imprecise_dtypes=True),
           writes=[bME])
        op("gp", lambda g: g.iota(ME2, pattern=[[128, 16]], base=1 - 1024,
                                  channel_multiplier=1,
                                  allow_small_or_imprecise_dtypes=True),
           writes=[bME2])
        op("gp", lambda g: g.memset(ONESB, 1.0), writes=[bONESB])
        op("gp", lambda g: g.memset(EPSB, 1e-5), writes=[bEPSB])
        op("gp", lambda g: g.memset(ONES32, 1.0), writes=[bONES32])
        op("ve", lambda v: v.tensor_scalar(out=IDB, in0=IOTA, scalar1=0.0,
                                           scalar2=None, op0=AL.is_equal),
           reads=[bIOTA], writes=[bIDB])
        op("ve", lambda v: v.tensor_scalar(out=ID32, in0=IOTA, scalar1=0.0,
                                           scalar2=None, op0=AL.is_equal),
           reads=[bIOTA], writes=[bID32])
        op("ve", lambda v: v.tensor_scalar(out=TRI, in0=IOTA, scalar1=0.0,
                                           scalar2=-3.0e38, op0=AL.is_gt,
                                           op1=AL.mult),
           reads=[bIOTA], writes=[bTRI])
        op("sy", lambda s: s.dma_start(
            out=BQKS, in_=bqkv[0:2048].rearrange("(j p) -> p j", p=128),
            allow_slow_non_contiguous=True),
           writes=[bBQKS], dma=True)
        op("ve", lambda v: v.tensor_scalar_mul(BQKS[:, 0:8], BQKS[:, 0:8], 0.125),
           writes=[bBQKS])
        op("sy", lambda s: s.dma_start(out=PSCROW, in_=pscale[None, :]),
           writes=[bPSCROW], dma=True)
        op("gp", lambda g: g.dma_start(out=BVBO[:, 0:1024],
                                       in_=bqkv[None, 2048:3072]),
           writes=[bBVBO], dma=True)
        op("gp", lambda g: g.dma_start(out=BVBO[:, 1024:2048],
                                       in_=boutw[None, :]),
           writes=[bBVBO], dma=True)
        op("pe", lambda t: t.matmul(bk(7)[:, 0:16], ONES32, PSCROW,
                                    start=True, stop=True),
           reads=[bONES32, bPSCROW], writes=[psb[7]])
        op("ve", lambda v: v.tensor_copy(PSCB, bk(7)[:, 0:16]),
           reads=[psb[7]], writes=[bPSCB])
        op("ve", lambda v: v.tensor_copy(
            PSC6.rearrange("p (h c) -> p h c", h=16),
            PSCB[:, :, None].broadcast_to([128, 16, 6])),
           reads=[bPSCB], writes=[bPSC6])
        GST = a32(45056, 1024)
        bGST = Buf()
        op("sy", lambda s: s.dma_start(out=GST[0:16, :], in_=gate),
           writes=[bGST], dma=True)
        for c in range(8):
            b = c % 2
            op("pe", lambda t, c=c, b=b: t.transpose(
                bk(b)[:, 0:16], GST[0:16, c * 128:(c + 1) * 128],
                ID32[0:16, 0:16]),
               reads=[bGST, bID32], writes=[psb[b]])
            op("ve", lambda v, c=c, b=b: v.tensor_copy(
                GATET[:, c * 16:(c + 1) * 16], bk(b)[:, 0:16]),
               reads=[psb[b]], writes=[bGATET])

        K.barrier()

        nstore = [0]

        for s in range(NSEQ):
            xs = xin[s * T:(s + 1) * T, :]
            ys = yout[s * T:(s + 1) * T, :]

            XT = a16(0, 8192)
            bXT = Buf()
            xtok = [a16(45056, 2048), a16(47104, 2048)]
            bxtok = [Buf(), Buf()]
            for tc in range(4):
                sl = tc % 2
                op("gp", lambda g, tc=tc, sl=sl, xs=xs: g.dma_start(
                    out=xtok[sl].rearrange("p (t d) -> p t d", t=4),
                    in_=xs[tc * 512:(tc + 1) * 512, :]
                    .rearrange("(t p) d -> p t d", p=128)),
                   writes=[bxtok[sl]], dma=True)
                for c in range(8):
                    b = c % 2
                    for t4 in range(4):
                        op("pe", lambda t, c=c, t4=t4, b=b, sl=sl: t.transpose(
                            bkb(b)[:, t4 * 128:(t4 + 1) * 128],
                            xtok[sl][:, (t4 * 8 + c) * 128:(t4 * 8 + c + 1) * 128],
                            IDB),
                           reads=[bxtok[sl], bIDB], writes=[psb[b]])
                    op("ve", lambda v, c=c, tc=tc, b=b: v.tensor_copy(
                        XT[:, c * 2048 + tc * 512:c * 2048 + tc * 512 + 512],
                        bkb(b)[:, 0:512]),
                       reads=[psb[b]], writes=[bXT])

            K.barrier()

            LTQ = a16(45056, 288)
            LTK = a16(45344, 288)
            P12 = a32(45632, 128)
            WKA = a32(45760, 96)
            WKB = a32(45856, 96)
            WKC = a32(45952, 96)
            WKD = a32(46048, 96)
            WKR = a32(46144, 96)
            WKN = a32(46240, 96)
            WSM = a32(46336, 48)
            WPB = a16(46480, 512)
            LINQ = a16(32768, 6144)
            LINK = a16(38912, 6144)
            bLTQ = Buf(); bLTK = Buf(); bP12 = Buf()
            bWKA = Buf(); bWKB = Buf(); bWKC = Buf(); bWKD = Buf()
            bWKR = Buf(); bWKN = Buf(); bWSM = Buf(); bWPB = Buf()
            bLINQ = Buf(); bLINK = Buf()

            op("ve", lambda v: v.memset(LTQ, 0.0), writes=[bLTQ])
            op("ve", lambda v: v.memset(LTK, 0.0), writes=[bLTK])
            op("gp", lambda g: g.memset(LINQ, 0.0), writes=[bLINQ])
            op("gp", lambda g: g.memset(LINK, 0.0), writes=[bLINK])
            op("gp", lambda g: g.dma_start(
                out=WPB.rearrange("p (c m) -> p c m", c=8)[:, :, 0:64],
                in_=w1p.rearrange("(c k) m -> k c m", c=8)),
               writes=[bWPB], dma=True)
            op("gp", lambda g: g.dma_start(
                out=WPB.rearrange("p (c m) -> p c m", c=8)[:, :, 64:128],
                in_=w2p.rearrange("(c k) m -> k c m", c=8)),
               writes=[bWPB], dma=True)

            for tt in range(NT):
                for c in range(8):
                    op("pe", lambda t, c=c, tt=tt: t.matmul(
                        bk(0)[:, 0:128],
                        XT[:, c * 2048 + tt * 128:c * 2048 + tt * 128 + 128],
                        WPB[:, c * 128:(c + 1) * 128],
                        start=(c == 0), stop=(c == 7)),
                       reads=[bXT, bWPB], writes=[psb[0]])
                op("ve", lambda v: v.tensor_copy(P12, bk(0)[:, 0:128]),
                   reads=[psb[0]], writes=[bP12])
                p1 = P12.rearrange("p (h c) -> p h c", h=16)
                p2 = P12[:, 64:128].rearrange("p (h c) -> p h c", h=16)
                vA = WKA.rearrange("p (h c) -> p h c", h=16)
                vB = WKB.rearrange("p (h c) -> p h c", h=16)
                vC = WKC.rearrange("p (h c) -> p h c", h=16)
                vD = WKD.rearrange("p (h c) -> p h c", h=16)
                for (dst, bdst, src) in ((vA, bWKA, p1), (vD, bWKD, p2)):
                    op("ve", lambda v, dst=dst, src=src: v.tensor_copy(
                        dst[:, :, 0:3],
                        src[:, :, 0:1].broadcast_to([128, 16, 3])),
                       reads=[bP12], writes=[bdst])
                    op("ve", lambda v, dst=dst, src=src: v.tensor_copy(
                        dst[:, :, 3:5],
                        src[:, :, 1:2].broadcast_to([128, 16, 2])),
                       reads=[bP12], writes=[bdst])
                    op("ve", lambda v, dst=dst, src=src: v.tensor_copy(
                        dst[:, :, 5:6], src[:, :, 2:3]),
                       reads=[bP12], writes=[bdst])
                for (dst, bdst, src) in ((vB, bWKB, p2), (vC, bWKC, p1)):
                    op("ve", lambda v, dst=dst, src=src: v.tensor_copy(
                        dst[:, :, 0:3], src[:, :, 1:4]),
                       reads=[bP12], writes=[bdst])
                    op("ve", lambda v, dst=dst, src=src: v.tensor_copy(
                        dst[:, :, 3:5], src[:, :, 2:4]),
                       reads=[bP12], writes=[bdst])
                    op("ve", lambda v, dst=dst, src=src: v.tensor_copy(
                        dst[:, :, 5:6], src[:, :, 3:4]),
                       reads=[bP12], writes=[bdst])
                op("ve", lambda v: v.tensor_mul(WKR, WKA, WKB),
                   reads=[bWKA, bWKB], writes=[bWKR])
                op("ve", lambda v: v.tensor_mul(WKA, WKC, WKD),
                   reads=[bWKC, bWKD], writes=[bWKA])
                op("ve", lambda v: v.tensor_sub(WKR, WKR, WKA),
                   reads=[bWKA], writes=[bWKR])
                op("ve", lambda v: v.tensor_mul(WKA, WKR, WKR),
                   reads=[bWKR], writes=[bWKA])
                op("ve", lambda v: v.reduce_sum(
                    WSM[:, 0:16], WKA.rearrange("p (h c) -> p h c", h=16),
                    axis=AX), reads=[bWKA], writes=[bWSM])
                op("ae", lambda a: a.activation(WSM[:, 16:32], WSM[:, 0:16],
                                                AF.Sqrt),
                   reads=[bWSM], writes=[bWSM])
                op("ve", lambda v: v.tensor_scalar_max(
                    WSM[:, 16:32], WSM[:, 16:32], 1e-12), writes=[bWSM])
                op("ve", lambda v: v.reciprocal(WSM[:, 32:48], WSM[:, 16:32]),
                   writes=[bWSM])
                op("ve", lambda v: v.tensor_mul(
                    WKN.rearrange("p (h c) -> p h c", h=16),
                    WKR.rearrange("p (h c) -> p h c", h=16),
                    WSM[:, 32:48][:, :, None].broadcast_to([128, 16, 6])),
                   reads=[bWKR, bWSM], writes=[bWKN])
                op("ve", lambda v: v.tensor_mul(WKR, WKN, PSC6),
                   reads=[bWKN, bPSC6], writes=[bWKR])
                vN = WKN.rearrange("p (h c) -> p h c", h=16)
                vR = WKR.rearrange("p (h c) -> p h c", h=16)
                ltq = LTQ.rearrange("p (g a w) -> p g a w", g=6, a=3)
                ltk = LTK.rearrange("p (g a w) -> p g a w", g=6, a=3)
                op("ve", lambda v: v.tensor_copy(
                    ltq[:, 0:5, :, 0:6],
                    vR[:, 0:15, :].rearrange("p (g a) c -> p g a c", g=5)),
                   reads=[bWKR], writes=[bLTQ])
                op("ve", lambda v: v.tensor_copy(
                    ltq[:, 5:6, 0:1, 0:6], vR[:, 15:16, None, :]),
                   reads=[bWKR], writes=[bLTQ])
                op("ve", lambda v: v.tensor_copy(
                    ltk[:, 0:5, :, 0:6],
                    vN[:, 0:15, ::-1].rearrange("p (g a) c -> p g a c", g=5)),
                   reads=[bWKN], writes=[bLTK])
                op("ve", lambda v: v.tensor_copy(
                    ltk[:, 5:6, 0:1, 0:6], vN[:, 15:16, None, ::-1]),
                   reads=[bWKN], writes=[bLTK])
                op("ve", lambda v: v.tensor_scalar_mul(
                    ltk[:, :, :, 1:2], ltk[:, :, :, 1:2], -1.0), writes=[bLTK])
                op("ve", lambda v: v.tensor_scalar_mul(
                    ltk[:, :, :, 4:5], ltk[:, :, :, 4:5], -1.0), writes=[bLTK])
                for g in range(6):
                    for (src, bsrc, dst, bdst) in ((LTQ, bLTQ, LINQ, bLINQ),
                                                   (LTK, bLTK, LINK, bLINK)):
                        b = 2 + (g % 2)
                        op("pe", lambda t, g=g, src=src, b=b: t.transpose(
                            bkb(b)[0:96, 0:128],
                            src[:, g * 96:(g + 1) * 96], IDB),
                           reads=[bsrc, bIDB], writes=[psb[b]])
                        op("ve", lambda v, g=g, tt=tt, dst=dst, b=b:
                           v.tensor_copy(
                               dst[0:96, g * 2048 + tt * 128:
                                   g * 2048 + tt * 128 + 128],
                               bkb(b)[0:96, 0:128]),
                           reads=[psb[b]], writes=[bdst])

            K.barrier()

            QT = a16(8192, 8192)
            KT = a16(16384, 8192)
            bQT = Buf(); bKT = Buf()
            WQK = [a16(45056, 512), a16(45568, 512)]
            bWQK = [Buf(), Buf()]
            for pb in range(16):
                sl = pb % 2
                op("gp", lambda g, pb=pb, sl=sl: g.dma_start(
                    out=WQK[sl].rearrange("p (c m) -> p c m", c=8),
                    in_=wqkv[:, pb * 128:(pb + 1) * 128]
                    .rearrange("(c k) m -> k c m", c=8)),
                   writes=[bWQK[sl]], dma=True)
                dst = QT if pb < 8 else KT
                bdst = bQT if pb < 8 else bKT
                scale = 0.125 if pb < 8 else 1.0
                for tcb in range(4):
                    b = 4 + (tcb % 2)
                    for c in range(8):
                        op("pe", lambda t, pb=pb, sl=sl, tcb=tcb, c=c, b=b:
                           t.matmul(
                               bk(b), WQK[sl][:, c * 128:(c + 1) * 128],
                               XT[:, c * 2048 + tcb * 512:
                                  c * 2048 + tcb * 512 + 512],
                               start=(c == 0), stop=(c == 7)),
                           reads=[bWQK[sl], bXT], writes=[psb[b]])
                    op("ae", lambda a, pb=pb, tcb=tcb, b=b, dst=dst,
                       scale=scale: a.activation(
                           dst[:, (pb % 8) * 2048 + tcb * 512:
                               (pb % 8) * 2048 + tcb * 512 + 512],
                           bk(b), AF.Identity, bias=BQKS[:, pb:pb + 1],
                           scale=scale),
                       reads=[psb[b], bBQKS], writes=[bdst])

            V = a16(24576, 8192)
            bV = Buf()
            WV = a16(45056, 4096)
            bWV = Buf(reuse=bWQK)
            op("gp", lambda g: g.dma_start(
                out=WV.rearrange("p (c m) -> p c m", c=8),
                in_=wqkv[:, 2048:3072].rearrange("(c k) m -> k c m", c=8)),
               writes=[bWV], dma=True)
            for tt in range(NT):
                b0 = 0 if tt % 2 == 0 else 2
                for c in range(8):
                    for hf in range(2):
                        op("pe", lambda t, tt=tt, c=c, hf=hf, b0=b0: t.matmul(
                            bk(b0 + hf),
                            XT[:, c * 2048 + tt * 128:c * 2048 + tt * 128 + 128],
                            WV[:, c * 1024 + hf * 512:c * 1024 + hf * 512 + 512],
                            start=(c == 0), stop=False),
                           reads=[bXT, bWV], writes=[psb[b0 + hf]])
                for hf in range(2):
                    op("pe", lambda t, hf=hf, b0=b0: t.matmul(
                        bk(b0 + hf), ONESB,
                        BVBO[:, hf * 512:(hf + 1) * 512],
                        start=False, stop=True),
                       reads=[bONESB, bBVBO], writes=[psb[b0 + hf]])
                op("ve", lambda v, tt=tt, b0=b0: v.tensor_copy(
                    V[:, tt * 1024:(tt + 1) * 1024], bk(b0, 2)),
                   reads=[psb[b0], psb[b0 + 1]], writes=[bV])

            K.barrier()

            PB = [a16(0, 1024), a16(1024, 1024)]
            bPB = [Buf(), Buf()]
            PT = a16(2048, 4096)
            bPT = Buf()
            AOTQ = a16(6144, 2048)
            bAOTQ = Buf()
            WOUT = a16(45056, 4096)
            bWOUT = Buf(reuse=[bWV])
            XT32 = a32(49152, 1024)
            bXT32 = Buf()
            SH = [(0, psb[0], psb[1]), (2, psb[2], psb[3])]

            for qb in range(4):
                W = 512 * (qb + 1)
                if W <= 1024:
                    halves = [(0, W)]
                else:
                    halves = [(0, 1024), (1024, W - 1024)]
                nh = len(halves)
                nkc = W // 128

                op("gp", lambda g, qb=qb: g.dma_start(
                    out=WOUT.rearrange("p (c m) -> p c m", c=8),
                    in_=woutw.rearrange("(c k) m -> k c m", c=8)),
                   writes=[bWOUT], dma=True)

                prev_T = {}
                for h in range(16):
                    par = h % 2
                    pair = h // 2
                    g6 = h // 3
                    a3 = h % 3
                    for qt in range(4):
                        qg = qb * 4 + qt
                        Wv = 128 * qg + 128
                        if Wv <= 1024:
                            halves_ex = [(0, Wv)]
                        else:
                            halves_ex = [(0, 1024), (1024, Wv - 1024)]
                        assert len(halves_ex) == nh
                        for hi, (koff, wd) in enumerate(halves_ex):
                            shi = (qt % 2) if nh == 1 else hi
                            sb0 = SH[shi][0]
                            col = qt * 2 + hi
                            nkc_h = (wd + 511) // 512
                            cbufs = [SH[shi][1 + kc] for kc in range(nkc_h)]
                            for kc in range(nkc_h):
                                cw = min(512, wd - kc * 512)
                                op("pe", lambda t, par=par, pair=pair, qg=qg,
                                   koff=koff, kc=kc, sb0=sb0, cw=cw: t.matmul(
                                       bk(sb0 + kc)[:, 0:cw],
                                       QT[64 * par:64 * par + 64,
                                          pair * 2048 + qg * 128:
                                          pair * 2048 + qg * 128 + 128],
                                       KT[64 * par:64 * par + 64,
                                          pair * 2048 + koff + kc * 512:
                                          pair * 2048 + koff + kc * 512 + cw],
                                       start=True, stop=False),
                                   reads=[bQT, bKT], writes=[cbufs[kc]])
                            for kc in range(nkc_h):
                                cw = min(512, wd - kc * 512)
                                op("pe", lambda t, g6=g6, a3=a3, qg=qg,
                                   koff=koff, kc=kc, sb0=sb0, cw=cw: t.matmul(
                                       bk(sb0 + kc)[:, 0:cw],
                                       LINQ[32 * a3:32 * a3 + 8,
                                            g6 * 2048 + qg * 128:
                                            g6 * 2048 + qg * 128 + 128],
                                       LINK[32 * a3:32 * a3 + 8,
                                            g6 * 2048 + koff + kc * 512:
                                            g6 * 2048 + koff + kc * 512 + cw],
                                       start=False, stop=True),
                                   reads=[bLINQ, bLINK], writes=[cbufs[kc]])
                            if koff + wd == Wv:
                                op("ve", lambda v, sb0=sb0, wd=wd:
                                   v.tensor_add(
                                       bk(sb0, 2)[:, wd - 128:wd],
                                       bk(sb0, 2)[:, wd - 128:wd], TRI),
                                   reads=[bTRI, cbufs[-1]],
                                   writes=[cbufs[-1]])
                            op("ve", lambda v, sb0=sb0, wd=wd, col=col:
                               v.reduce_max(
                                   NRH[:, col:col + 1],
                                   bk(sb0, 2)[:, 0:wd], axis=AX, negate=True),
                               reads=cbufs, writes=[bNRH])
                            op("ae", lambda a, sb0=sb0, wd=wd, koff=koff,
                               qt=qt, col=col: a.activation(
                                   PB[qt % 2][:, koff:koff + wd],
                                   bk(sb0, 2)[:, 0:wd], AF.Exp,
                                   bias=NRH[:, col:col + 1], scale=1.0,
                                   accum_out=RSH[:, col:col + 1]),
                               reads=cbufs + [bNRH],
                               writes=[bRSH, bPB[qt % 2]])
                        if Wv < W:
                            op("ve", lambda v, qt=qt, Wv=Wv, W=W: v.memset(
                                PB[qt % 2][:, Wv:W], 0.0),
                               writes=[bPB[qt % 2]])
                        if qt % 2 == 1:
                            p0 = qt - 1
                            c0 = p0 * 2
                            if nh == 1:
                                op("ve", lambda v, c0=c0: v.reciprocal(
                                    EPI[:, 20:24].rearrange(
                                        "p (q i) -> p q i", q=2)[:, :, 0:1],
                                    RSH[:, c0:c0 + 4].rearrange(
                                        "p (q i) -> p q i", q=2)[:, :, 0:1]),
                                   reads=[bRSH], writes=[bEPI])
                            else:
                                nr4 = NRH[:, c0:c0 + 4].rearrange(
                                    "p (q i) -> p q i", q=2)
                                rs4 = RSH[:, c0:c0 + 4].rearrange(
                                    "p (q i) -> p q i", q=2)
                                e4 = EPI[:, 4:8].rearrange(
                                    "p (q i) -> p q i", q=2)
                                e8 = EPI[:, 8:12].rearrange(
                                    "p (q i) -> p q i", q=2)
                                e12 = EPI[:, 12:16].rearrange(
                                    "p (q i) -> p q i", q=2)
                                e20 = EPI[:, 20:24].rearrange(
                                    "p (q i) -> p q i", q=2)
                                op("ve", lambda v, nr4=nr4: v.tensor_tensor(
                                    EPI[:, 0:2][:, :, None],
                                    nr4[:, :, 0:1], nr4[:, :, 1:2], AL.min),
                                   reads=[bNRH], writes=[bEPI])
                                op("ve", lambda v, nr4=nr4, e4=e4:
                                   v.tensor_sub(
                                       e4, EPI[:, 0:2][:, :, None]
                                       .broadcast_to([128, 2, 2]), nr4),
                                   reads=[bNRH], writes=[bEPI])
                                op("ae", lambda a: a.activation(
                                    EPI[:, 8:12], EPI[:, 4:8], AF.Exp),
                                   reads=[bEPI], writes=[bEPI])
                                op("ve", lambda v, rs4=rs4, e8=e8, e12=e12:
                                   v.tensor_mul(e12, rs4, e8),
                                   reads=[bRSH, bEPI], writes=[bEPI])
                                op("ve", lambda v, e12=e12: v.tensor_add(
                                    EPI[:, 16:18][:, :, None],
                                    e12[:, :, 0:1], e12[:, :, 1:2]),
                                   reads=[bEPI], writes=[bEPI])
                                op("ve", lambda v: v.reciprocal(
                                    EPI[:, 18:20], EPI[:, 16:18]),
                                   reads=[bEPI], writes=[bEPI])
                                op("ve", lambda v, e8=e8, e20=e20:
                                   v.tensor_mul(
                                       e20, e8, EPI[:, 18:20][:, :, None]
                                       .broadcast_to([128, 2, 2])),
                                   reads=[bEPI], writes=[bEPI])
                            for qt2 in (p0, qt):
                                for hi, (koff, wd) in enumerate(halves):
                                    sc_col = 20 + (qt2 - p0) * 2 + hi
                                    op("ve", lambda v, qt2=qt2, koff=koff,
                                       wd=wd, sc_col=sc_col:
                                       v.tensor_scalar_mul(
                                           PB[qt2 % 2][:, koff:koff + wd],
                                           PB[qt2 % 2][:, koff:koff + wd],
                                           EPI[:, sc_col:sc_col + 1]),
                                       reads=[bEPI], writes=[bPB[qt2 % 2]])
                            for qt2 in (p0, qt):
                                tb = 4 + (qt2 % 2)
                                for kg in range((W // 128 + 3) // 4):
                                    kcs = [kc for kc in
                                           range(kg * 4, min(kg * 4 + 4,
                                                             W // 128))]
                                    for j, kc in enumerate(kcs):
                                        op("pe", lambda t, qt2=qt2, kc=kc,
                                           j=j, tb=tb: t.transpose(
                                               bkb(tb)[:, j * 128:(j + 1) * 128],
                                               PB[qt2 % 2][:, kc * 128:
                                                            (kc + 1) * 128],
                                               IDB),
                                           reads=[bPB[qt2 % 2], bIDB],
                                           writes=[psb[tb]])
                                    eng2 = "ve" if (kg + qt2) % 2 == 0 else "ae"
                                    if eng2 == "ve":
                                        op("ve", lambda v, qt2=qt2, kg=kg,
                                           kcs=kcs, tb=tb: v.tensor_copy(
                                               PT.rearrange(
                                                   "p (k q) -> p k q", q=512)
                                               [:, kg * 4:kg * 4 + len(kcs),
                                                qt2 * 128:qt2 * 128 + 128],
                                               bkb(tb)[:, 0:len(kcs) * 128]
                                               .rearrange("p (k q) -> p k q",
                                                          q=128)),
                                           reads=[psb[tb]], writes=[bPT])
                                    else:
                                        op("ae", lambda a, qt2=qt2, kg=kg,
                                           kcs=kcs, tb=tb: a.copy(
                                               out=PT.rearrange(
                                                   "p (k q) -> p k q", q=512)
                                               [:, kg * 4:kg * 4 + len(kcs),
                                                qt2 * 128:qt2 * 128 + 128],
                                               in_=bkb(tb)[:, 0:len(kcs) * 128]
                                               .rearrange("p (k q) -> p k q",
                                                          q=128)),
                                           reads=[psb[tb]], writes=[bPT])
                    ob = 6 + (pair % 2)
                    for kc in range(nkc):
                        op("pe", lambda t, h=h, kc=kc, par=par, ob=ob,
                           nkc=nkc: t.matmul(
                               PS[64 * par:64 * par + 64,
                                  ob * 512:ob * 512 + 512],
                               V[:, kc * 1024 + h * 64:kc * 1024 + h * 64 + 64],
                               PT[:, kc * 512:(kc + 1) * 512],
                               start=(kc == 0), stop=(kc == nkc - 1)),
                           reads=[bV, bPT], writes=[psb[ob]])
                    if par == 1:
                        op("ve", lambda v, pair=pair, ob=ob: v.tensor_copy(
                            AOTQ[:, pair * 512:(pair + 1) * 512], bk(ob)),
                           reads=[psb[ob]], writes=[bAOTQ])

                for ttl in range(4):
                    tt = qb * 4 + ttl
                    op("sy", lambda s_, tt=tt, xs=xs: s_.dma_start(
                        out=XT32,
                        in_=xs[tt * 128:(tt + 1) * 128, :]),
                       writes=[bXT32], dma=True)
                    for hf in range(2):
                        ob = 6 + hf
                        for pbk in range(8):
                            op("pe", lambda t, ttl=ttl, hf=hf, pbk=pbk, ob=ob:
                               t.matmul(
                                   bk(ob),
                                   AOTQ[:, pbk * 512 + ttl * 128:
                                        pbk * 512 + ttl * 128 + 128],
                                   WOUT[:, pbk * 1024 + hf * 512:
                                        pbk * 1024 + hf * 512 + 512],
                                   start=(pbk == 0), stop=False),
                               reads=[bAOTQ, bWOUT], writes=[psb[ob]])
                        op("pe", lambda t, hf=hf, ob=ob: t.matmul(
                            bk(ob), ONESB,
                            BVBO[:, 1024 + hf * 512:1024 + (hf + 1) * 512],
                            start=False, stop=True),
                           reads=[bONESB, bBVBO], writes=[psb[ob]])
                    op("ve", lambda v: v.tensor_add(XT32, XT32, bk(6, 2)),
                       reads=[psb[6], psb[7]], writes=[bXT32])
                    op("ve", lambda v: v.bn_stats(ST6[:, 0:6], XT32[:, 0:512]),
                       reads=[bXT32], writes=[bST6])
                    op("ve", lambda v: v.bn_stats(ST6[:, 6:12], XT32[:, 512:1024]),
                       reads=[bXT32], writes=[bST6])
                    op("ve", lambda v: v.bn_aggr(MV[:, 0:2], ST6),
                       reads=[bST6], writes=[bMVb])
                    op("ae", lambda a: a.activation(RT1[:, 0:1], MV[:, 1:2],
                                                    AF.Sqrt, bias=EPSB),
                       reads=[bMVb, bEPSB], writes=[bRT1])
                    op("ve", lambda v: v.reciprocal(RT1[:, 1:2], RT1[:, 0:1]),
                       writes=[bRT1])
                    op("ve", lambda v: v.scalar_tensor_tensor(
                        out=RT1[:, 2:3], in0=MV[:, 0:1], scalar=-1.0,
                        in1=RT1[:, 1:2], op0=AL.mult, op1=AL.mult),
                       reads=[bMVb], writes=[bRT1])
                    op("ae", lambda a: a.activation(
                        XT32, XT32, AF.Identity, bias=RT1[:, 2:3],
                        scale=RT1[:, 1:2]),
                       reads=[bRT1], writes=[bXT32])
                    op("ae", lambda a, tt=tt: a.dma_start(
                        out=h1sp[tt * 128:(tt + 1) * 128, :], in_=XT32),
                       reads=[bXT32], writes=[bH1SP], dma=True)

            K.barrier()

            H1T = a16(0, 8192)
            bH1T = Buf(reuse=[bPB[0], bPB[1], bPT, bAOTQ])
            MOE = a32(8192, 16384)
            bMOE = Buf(reuse=[bQT, bKT])

            def moe_t(tt):
                return a32(8192 + tt * 1024, 1024)

            G32 = a32(24576, 4096)
            bG32 = Buf(reuse=[bV])
            H1F = a32(28672, 4096)
            bH1F = Buf(reuse=[bV])
            H1B = a16(36096, 2048)
            bH1B = Buf(reuse=[bLINQ])
            COMBT = a16(32768, 1024)
            bCOMBT = Buf(reuse=[bLINQ])
            SCT = a32(33792, 2048)
            bSCT = Buf(reuse=[bLINQ])
            SCTOK = a32(35840, 256)
            bSCTOK = Buf(reuse=[bLINQ])
            COMBW = a32(38144, 16)
            bCOMBW = Buf(reuse=[bLINQ])

            for tc in range(4):
                op("sy", lambda s_, tc=tc: s_.dma_start(
                    out=H1F.rearrange("p (t d) -> p t d", t=4),
                    in_=h1sp[tc * 512:(tc + 1) * 512, :]
                    .rearrange("(t p) d -> p t d", p=128)),
                   reads=[bH1SP], writes=[bH1F], dma=True)
                op("gp", lambda g, tc=tc: g.dma_start(
                    out=H1B.rearrange("p (t d) -> p t d", t=4),
                    in_=h1sp[tc * 512:(tc + 1) * 512, :]
                    .rearrange("(t p) d -> p t d", p=128)),
                   reads=[bH1SP], writes=[bH1B], dma=True)
                for t4 in range(4):
                    tt = tc * 4 + t4
                    op("ve", lambda v, t4=t4, tt=tt: v.tensor_copy(
                        moe_t(tt), H1F[:, t4 * 1024:(t4 + 1) * 1024]),
                       reads=[bH1F], writes=[bMOE])
                    for c in range(8):
                        op("pe", lambda t, t4=t4, c=c: t.transpose(
                            bkb(4)[:, c * 128:(c + 1) * 128],
                            H1B[:, t4 * 1024 + c * 128:t4 * 1024 + c * 128 + 128],
                            IDB),
                           reads=[bH1B, bIDB], writes=[psb[4]])
                    op("ve", lambda v, tt=tt: v.tensor_copy(
                        H1T.rearrange("p (c q) -> p c q", c=8)
                        [:, :, tt * 128:tt * 128 + 128],
                        bkb(4)[:, 0:1024].rearrange("p (c q) -> p c q", c=8)),
                       reads=[psb[4]], writes=[bH1T])
                    for c in range(8):
                        b = 5 if c < 4 else 6
                        op("pe", lambda t, t4=t4, c=c, b=b: t.transpose(
                            bk(b)[:, (c % 4) * 128:(c % 4) * 128 + 128],
                            H1F[:, t4 * 1024 + c * 128:t4 * 1024 + c * 128 + 128],
                            ID32),
                           reads=[bH1F, bID32], writes=[psb[b]])
                    op("ve", lambda v, t4=t4: v.tensor_copy(
                        G32.rearrange("p (c q) -> p c q", c=8)
                        [:, 0:4, t4 * 128:t4 * 128 + 128],
                        bk(5)[:, 0:512].rearrange("p (c q) -> p c q", c=4)),
                       reads=[psb[5]], writes=[bG32])
                    op("ve", lambda v, t4=t4: v.tensor_copy(
                        G32.rearrange("p (c q) -> p c q", c=8)
                        [:, 4:8, t4 * 128:t4 * 128 + 128],
                        bk(6)[:, 0:512].rearrange("p (c q) -> p c q", c=4)),
                       reads=[psb[6]], writes=[bG32])
                for c in range(8):
                    op("pe", lambda t, c=c: t.matmul(
                        PS[0:16, 7 * 512:7 * 512 + 512],
                        GATET[:, c * 16:(c + 1) * 16],
                        G32[:, c * 512:(c + 1) * 512],
                        start=(c == 0), stop=(c == 7)),
                       reads=[bGATET, bG32], writes=[psb[7]])
                op("ae", lambda a, tc=tc: a.activation(
                    SCT[0:16, tc * 512:(tc + 1) * 512],
                    PS[0:16, 7 * 512:7 * 512 + 512], AF.Tanh, scale=0.5),
                   reads=[psb[7]], writes=[bSCT])
                op("ve", lambda v, tc=tc: v.tensor_scalar(
                    out=SCT[0:16, tc * 512:(tc + 1) * 512],
                    in0=SCT[0:16, tc * 512:(tc + 1) * 512],
                    scalar1=0.5, scalar2=0.5, op0=AL.mult, op1=AL.add),
                   writes=[bSCT])
                for t4 in range(4):
                    tt = tc * 4 + t4
                    op("pe", lambda t, tc=tc, t4=t4: t.transpose(
                        bk(4)[:, 0:16],
                        SCT[0:16, (tc * 4 + t4) * 128:(tc * 4 + t4) * 128 + 128],
                        ID32[0:16, 0:16]),
                       reads=[bSCT, bID32], writes=[psb[4]])
                    op("ve", lambda v, tt=tt: v.tensor_copy(
                        SCTOK[:, tt * 16:(tt + 1) * 16], bk(4)[:, 0:16]),
                       reads=[psb[4]], writes=[bSCTOK])

            for tt in range(NT):
                sc3 = SCTOK[:, tt * 16:(tt + 1) * 16] \
                    .rearrange("p (g e) -> p g e", g=4)
                A6 = RSC[:, 0:24].rearrange("p (g c) -> p g c", g=4)
                B6 = RSC[:, 24:48].rearrange("p (g c) -> p g c", g=4)
                op("ve", lambda v, sc3=sc3, A6=A6: v.tensor_copy(
                    A6[:, :, 0:3], sc3[:, :, 0:1].broadcast_to([128, 4, 3])),
                   reads=[bSCTOK], writes=[bRSC])
                op("ve", lambda v, sc3=sc3, A6=A6: v.tensor_copy(
                    A6[:, :, 3:5], sc3[:, :, 1:2].broadcast_to([128, 4, 2])),
                   reads=[bSCTOK], writes=[bRSC])
                op("ve", lambda v, sc3=sc3, A6=A6: v.tensor_copy(
                    A6[:, :, 5:6], sc3[:, :, 2:3]),
                   reads=[bSCTOK], writes=[bRSC])
                op("ve", lambda v, sc3=sc3, B6=B6: v.tensor_copy(
                    B6[:, :, 0:3], sc3[:, :, 1:4]),
                   reads=[bSCTOK], writes=[bRSC])
                op("ve", lambda v, sc3=sc3, B6=B6: v.tensor_copy(
                    B6[:, :, 3:5], sc3[:, :, 2:4]),
                   reads=[bSCTOK], writes=[bRSC])
                op("ve", lambda v, sc3=sc3, B6=B6: v.tensor_copy(
                    B6[:, :, 5:6], sc3[:, :, 3:4]),
                   reads=[bSCTOK], writes=[bRSC])
                op("ve", lambda v, A6=A6, B6=B6: v.tensor_add(A6, A6, B6),
                   writes=[bRSC])
                op("ve", lambda v, A6=A6: v.reduce_max(
                    RSC[:, 48:52], A6, axis=AX), writes=[bRSC])
                op("ve", lambda v: v.memset(RSC[:, 52:60], -FMAXP),
                   writes=[bRSC])
                op("ve", lambda v: v.tensor_copy(RSC[:, 52:56], RSC[:, 48:52]),
                   writes=[bRSC])
                op("ve", lambda v: v.max(out=MV[:, 0:8], in_=RSC[:, 52:60]),
                   reads=[bRSC], writes=[bMVb])
                op("ve", lambda v: v.tensor_scalar(
                    out=RSC[:, 60:64], in0=RSC[:, 48:52],
                    scalar1=MV[:, 1:2], scalar2=None, op0=AL.is_ge),
                   reads=[bMVb], writes=[bRSC])
                op("ve", lambda v, sc3=sc3: v.tensor_mul(
                    RSC[:, 0:16].rearrange("p (g e) -> p g e", g=4), sc3,
                    RSC[:, 60:64][:, :, None].broadcast_to([128, 4, 4])),
                   reads=[bSCTOK], writes=[bRSC])
                op("ve", lambda v: v.max(out=MV[:, 0:8], in_=RSC[:, 0:16]),
                   reads=[bRSC], writes=[bMVb])
                op("ve", lambda v: v.tensor_scalar(
                    out=RSC[:, 16:32], in0=RSC[:, 0:16],
                    scalar1=MV[:, 3:4], scalar2=None, op0=AL.is_ge),
                   reads=[bMVb], writes=[bRSC])
                op("ve", lambda v: v.tensor_mul(
                    RSC[:, 32:48], RSC[:, 0:16], RSC[:, 16:32]),
                   writes=[bRSC])
                op("ve", lambda v: v.reduce_sum(
                    RSC[:, 64:65], RSC[:, 32:48], axis=AX), writes=[bRSC])
                op("ve", lambda v: v.tensor_scalar_add(
                    RSC[:, 64:65], RSC[:, 64:65], 1e-20), writes=[bRSC])
                op("ve", lambda v: v.reciprocal(RSC[:, 65:66], RSC[:, 64:65]),
                   writes=[bRSC])
                op("ve", lambda v: v.tensor_scalar_mul(
                    RSC[:, 65:66], RSC[:, 65:66], 2.5), writes=[bRSC])
                op("ve", lambda v: v.tensor_scalar_mul(
                    COMBW, RSC[:, 32:48], RSC[:, 65:66]),
                   reads=[bRSC], writes=[bCOMBW])
                op("pe", lambda t: t.transpose(bk(5)[0:16, 0:128], COMBW, ID32),
                   reads=[bCOMBW, bID32], writes=[psb[5]])
                op("ve", lambda v, tt=tt: v.tensor_copy(
                    COMBT[0:16, tt * 128:tt * 128 + 128],
                    bk(5)[0:16, 0:128]),
                   reads=[psb[5]], writes=[bCOMBT])

            K.barrier()

            SHUP = [a16(24576, 2048), a16(26624, 2048)]
            bSHUP = [Buf(reuse=[bG32, bH1F]), Buf(reuse=[bG32, bH1F])]
            SHDN = [a16(28672, 2048), a16(30720, 2048)]
            bSHDN = [Buf(reuse=[bG32, bH1F]), Buf(reuse=[bG32, bH1F])]
            SHS = a16(38912, 4096)
            bSHS = Buf(reuse=[bLINK])

            for iq in range(4):
                sl = iq % 2
                op("gp", lambda g, iq=iq, sl=sl: g.dma_start(
                    out=SHUP[sl].rearrange("p (c m) -> p c m", c=8),
                    in_=shupw[:, iq * 512:(iq + 1) * 512]
                    .rearrange("(c k) m -> k c m", c=8)),
                   writes=[bSHUP[sl]], dma=True)
                op("gp", lambda g, iq=iq, sl=sl: g.dma_start(
                    out=SHDN[sl].rearrange("p (c m) -> p c m", c=4),
                    in_=shdnw[iq * 512:(iq + 1) * 512, :]
                    .rearrange("(c k) m -> k c m", c=4)),
                   writes=[bSHDN[sl]], dma=True)
                for i2c in range(4):
                    for tcb in range(4):
                        b = (i2c * 4 + tcb) % 4
                        for c in range(8):
                            op("pe", lambda t, sl=sl, i2c=i2c, tcb=tcb,
                               c=c, b=b: t.matmul(
                                   bk(b),
                                   SHUP[sl][:, c * 512 + i2c * 128:
                                            c * 512 + i2c * 128 + 128],
                                   H1T[:, c * 2048 + tcb * 512:
                                       c * 2048 + tcb * 512 + 512],
                                   start=(c == 0), stop=(c == 7)),
                               reads=[bSHUP[sl], bH1T], writes=[psb[b]])
                        op("ae", lambda a, i2c=i2c, tcb=tcb, b=b:
                           a.activation(
                               SHS[:, i2c * 2048 + tcb * 512:
                                   i2c * 2048 + tcb * 512 + 512],
                               bk(b), AF.Silu),
                           reads=[psb[b]], writes=[bSHS])
                for tt in range(NT):
                    b0 = 4 if tt % 2 == 0 else 6
                    for i2c in range(4):
                        for hf in range(2):
                            op("pe", lambda t, sl=sl, i2c=i2c, tt=tt, hf=hf,
                               b0=b0: t.matmul(
                                   bk(b0 + hf),
                                   SHS[:, i2c * 2048 + tt * 128:
                                       i2c * 2048 + tt * 128 + 128],
                                   SHDN[sl][:, i2c * 1024 + hf * 512:
                                            i2c * 1024 + hf * 512 + 512],
                                   start=(i2c == 0), stop=(i2c == 3)),
                               reads=[bSHS, bSHDN[sl]], writes=[psb[b0 + hf]])
                    op("ve", lambda v, tt=tt, b0=b0: v.tensor_add(
                        moe_t(tt), moe_t(tt), bk(b0, 2)),
                       reads=[psb[b0], psb[b0 + 1]], writes=[bMOE])

            WUP = [a16(24576, 2048), a16(26624, 2048)]
            bWUP = [Buf(reuse=[bSHUP[0]]), Buf(reuse=[bSHUP[1]])]
            WDN = [a16(28672, 2048), a16(30720, 2048)]
            bWDN = [Buf(reuse=[bSHDN[0]]), Buf(reuse=[bSHDN[1]])]
            SILU = a16(38912, 4096)
            bSILU = Buf(reuse=[bSHS])
            COMBE = [a16(33792, 1024), a16(34816, 1024)]
            bCOMBE = [Buf(reuse=[bSCT, bSCTOK, bH1B]),
                      Buf(reuse=[bSCT, bSCTOK, bH1B])]
            COMBB = [a16(43008, 1024), a16(44032, 1024)]
            bCOMBB = [Buf(reuse=[bLINK]), Buf(reuse=[bLINK])]

            for e in range(E):
                sl = e % 2
                op("gp", lambda g, e=e, sl=sl: g.dma_start(
                    out=WUP[sl].rearrange("p (c m) -> p c m", c=8),
                    in_=eup[e].rearrange("(c k) m -> k c m", c=8)),
                   writes=[bWUP[sl]], dma=True)
                op("gp", lambda g, e=e, sl=sl: g.dma_start(
                    out=WDN[sl].rearrange("p (c m) -> p c m", c=4),
                    in_=edn[e].rearrange("(c k) m -> k c m", c=4)),
                   writes=[bWDN[sl]], dma=True)
                op("sy", lambda s_, e=e, sl=sl: s_.dma_start(
                    out=COMBE[sl][0:1, :], in_=COMBT[0:16, :][e:e + 1, :]),
                   reads=[bCOMBT], writes=[bCOMBE[sl]], dma=True)
                for tcb in range(4):
                    b = tcb % 2
                    op("pe", lambda t, sl=sl, tcb=tcb, b=b: t.matmul(
                        bk(b), ONESB, COMBE[sl][0:1, tcb * 512:(tcb + 1) * 512],
                        start=True, stop=True),
                       reads=[bONESB, bCOMBE[sl]], writes=[psb[b]])
                    op("ve", lambda v, sl=sl, tcb=tcb, b=b: v.tensor_copy(
                        COMBB[sl][:, tcb * 512:(tcb + 1) * 512], bk(b)),
                       reads=[psb[b]], writes=[bCOMBB[sl]])
                for ic in range(4):
                    for tcb in range(4):
                        b = (ic * 4 + tcb) % 4
                        for c in range(8):
                            op("pe", lambda t, sl=sl, ic=ic, tcb=tcb, c=c,
                               b=b: t.matmul(
                                   bk(b),
                                   WUP[sl][:, c * 512 + ic * 128:
                                           c * 512 + ic * 128 + 128],
                                   H1T[:, c * 2048 + tcb * 512:
                                       c * 2048 + tcb * 512 + 512],
                                   start=(c == 0), stop=(c == 7)),
                               reads=[bWUP[sl], bH1T], writes=[psb[b]])
                        op("ae", lambda a, ic=ic, tcb=tcb, b=b: a.activation(
                            SILU[:, ic * 2048 + tcb * 512:
                                 ic * 2048 + tcb * 512 + 512],
                            bk(b), AF.Silu),
                           reads=[psb[b]], writes=[bSILU])
                op("gp", lambda g, sl=sl: g.tensor_mul(
                    SILU.rearrange("p (i q) -> p i q", i=4),
                    SILU.rearrange("p (i q) -> p i q", i=4),
                    COMBB[sl][:, None, :].broadcast_to([128, 4, 2048])),
                   reads=[bCOMBB[sl]], writes=[bSILU])
                for tt in range(NT):
                    b0 = 4 if tt % 2 == 0 else 6
                    for ic in range(4):
                        for hf in range(2):
                            op("pe", lambda t, sl=sl, ic=ic, tt=tt, hf=hf,
                               b0=b0: t.matmul(
                                   bk(b0 + hf),
                                   SILU[:, ic * 2048 + tt * 128:
                                        ic * 2048 + tt * 128 + 128],
                                   WDN[sl][:, ic * 1024 + hf * 512:
                                           ic * 1024 + hf * 512 + 512],
                                   start=(ic == 0), stop=(ic == 3)),
                               reads=[bSILU, bWDN[sl]], writes=[psb[b0 + hf]])
                    op("ve", lambda v, tt=tt, b0=b0: v.tensor_add(
                        moe_t(tt), moe_t(tt), bk(b0, 2)),
                       reads=[psb[b0], psb[b0 + 1]], writes=[bMOE])

            OUTB = [a32(0, 1024), a32(1024, 1024)]
            bOUT = [Buf(reuse=[bH1T]), Buf(reuse=[bH1T])]
            for tt in range(NT):
                sl = tt % 2
                op("ve", lambda v, tt=tt: v.bn_stats(
                    ST6[:, 0:6], moe_t(tt)[:, 0:512]),
                   reads=[bMOE], writes=[bST6])
                op("ve", lambda v, tt=tt: v.bn_stats(
                    ST6[:, 6:12], moe_t(tt)[:, 512:1024]),
                   reads=[bMOE], writes=[bST6])
                op("ve", lambda v: v.bn_aggr(MV[:, 0:2], ST6),
                   reads=[bST6], writes=[bMVb])
                op("ae", lambda a: a.activation(RT1[:, 0:1], MV[:, 1:2],
                                                AF.Sqrt, bias=EPSB),
                   reads=[bMVb, bEPSB], writes=[bRT1])
                op("ve", lambda v: v.reciprocal(RT1[:, 1:2], RT1[:, 0:1]),
                   writes=[bRT1])
                op("ve", lambda v: v.scalar_tensor_tensor(
                    out=RT1[:, 2:3], in0=MV[:, 0:1], scalar=-1.0,
                    in1=RT1[:, 1:2], op0=AL.mult, op1=AL.mult),
                   reads=[bMVb], writes=[bRT1])
                op("ae", lambda a, tt=tt, sl=sl: a.activation(
                    OUTB[sl], moe_t(tt), AF.Identity, bias=RT1[:, 2:3],
                    scale=RT1[:, 1:2]),
                   reads=[bRT1, bMOE], writes=[bOUT[sl]])
                op("ae", lambda a, tt=tt, sl=sl, ys=ys: a.dma_start(
                    out=ys[tt * 128:(tt + 1) * 128, :], in_=OUTB[sl]),
                   reads=[bOUT[sl]], dma=True)
                nstore[0] += 1

            K.barrier()

        dae = K.dsem["ae"]
        K.prog["ae"].append(([(dae, dae.val)], None, None, 0))
        dgp = K.dsem["gp"]
        K.prog["gp"].append(([(dgp, dgp.val)], None, None, 0))
        dsy = K.dsem["sy"]
        K.prog["sy"].append(([(dsy, dsy.val)], None, None, 0))

        with nc.Block() as block:
            @block.tensor
            def _(t):
                K.run({"pe": t})

            @block.vector
            def _(v):
                K.run({"ve": v})

            @block.scalar
            def _(a):
                K.run({"ae": a})

            @block.gpsimd
            def _(g):
                K.run({"gp": g})

            @block.sync
            def _(sy):
                K.run({"sy": sy})

    return nc


_NC_CACHE = [None]


def kernel(L0_attn_x, L0_attn_Wqkv, L0_attn_bqkv, L0_attn_W1p, L0_attn_W2p,
           L0_attn_plucker_scale, L0_attn_Wout, L0_attn_bout,
           L0_ffn_gate_weight, L0_ffn_expert_up, L0_ffn_expert_down,
           L0_ffn_shared_up, L0_ffn_shared_down):
    x = np.ascontiguousarray(np.asarray(L0_attn_x, dtype=np.float32))
    B, Tq, Dm = x.shape
    assert (B, Tq, Dm) == (16, T, D)
    w = {
        "wqkv": np.ascontiguousarray(np.asarray(L0_attn_Wqkv, np.float32)),
        "bqkv": np.ascontiguousarray(np.asarray(L0_attn_bqkv, np.float32)),
        "w1p": np.ascontiguousarray(np.asarray(L0_attn_W1p, np.float32)),
        "w2p": np.ascontiguousarray(np.asarray(L0_attn_W2p, np.float32)),
        "pscale": np.ascontiguousarray(
            np.asarray(L0_attn_plucker_scale, np.float32)),
        "wout": np.ascontiguousarray(np.asarray(L0_attn_Wout, np.float32)),
        "bout": np.ascontiguousarray(np.asarray(L0_attn_bout, np.float32)),
        "gate": np.ascontiguousarray(
            np.asarray(L0_ffn_gate_weight, np.float32)),
        "eup": np.ascontiguousarray(np.asarray(L0_ffn_expert_up, np.float32)),
        "edn": np.ascontiguousarray(
            np.asarray(L0_ffn_expert_down, np.float32)),
        "shup": np.ascontiguousarray(
            np.asarray(L0_ffn_shared_up, np.float32)),
        "shdn": np.ascontiguousarray(
            np.asarray(L0_ffn_shared_down, np.float32)),
    }
    if _NC_CACHE[0] is None:
        _NC_CACHE[0] = build_program()
    nc = _NC_CACHE[0]
    in_maps = []
    for c in range(8):
        m = {"x": np.ascontiguousarray(
            x[2 * c:2 * c + 2].reshape(NSEQ * T, D))}
        m.update(w)
        in_maps.append(m)
    res = run_bass_kernel_spmd(nc, in_maps, core_ids=list(range(8)))
    out = np.concatenate(
        [r["y"].reshape(NSEQ, T, D) for r in res.results], axis=0)
    return out.astype(np.float32)


# version 05 (final)
# speedup vs baseline: 1.1254x; 1.1254x over previous
import numpy as np

import concourse.bass as bass
import concourse.mybir as mybir
from concourse.bass_utils import run_bass_kernel_spmd

F32 = mybir.dt.float32
BF16 = mybir.dt.bfloat16
AL = mybir.AluOpType
AF = mybir.ActivationFunctionType
AX = mybir.AxisListType.X

T = 2048
D = 1024
H = 16
E = 16
II = 512
IS = 2048
NT = 16
NSEQ = 2
FMAXP = 3.0e38

ARENA_N = 50176

ENGS = ("pe", "ve", "ae", "gp", "sy")
DEBUG = False


class Sem:
    def __init__(self, name):
        self.name = name
        self.val = 0
        self.h = None


class Buf:
    __slots__ = ("w", "r")

    def __init__(self, reuse=()):
        self.w = None
        self.r = []
        for ob in reuse:
            if ob.w is not None:
                self.r.append(ob.w)
            self.r.extend(ob.r)


class KB:
    def __init__(self):
        self.prog = {e: [] for e in ENGS}
        self.wm = {e: {} for e in ENGS}
        self.esem = {e: Sem(e) for e in ENGS}
        self.dsem = {"gp": Sem("dgp"), "sy": Sem("dsy"), "ae": Sem("dae")}
        self.bar = Sem("bar")

    def all_sems(self):
        return list(self.esem.values()) + list(self.dsem.values()) + [self.bar]

    def op(self, eng, fn, reads=(), writes=(), extra=(), dma=False):
        deps = []
        for b in reads:
            if b.w is not None:
                deps.append(b.w)
        for b in writes:
            if b.w is not None:
                deps.append(b.w)
            deps.extend(b.r)
        deps.extend(extra)
        best = {}
        for s, v in deps:
            if best.get(s.name, (None, -1))[1] < v:
                best[s.name] = (s, v)
        waits = []
        wm = self.wm[eng]
        for s, v in best.values():
            if wm.get(s.name, 0) < v:
                wm[s.name] = v
                waits.append((s, v))
        sem = self.dsem[eng] if dma else self.esem[eng]
        n = 16 if dma else 1
        sem.val += n
        tok = (sem, sem.val)
        self.prog[eng].append((waits, fn, sem, n))
        for b in reads:
            b.r.append(tok)
        for b in writes:
            b.w = tok
            b.r = []
        return tok

    def barrier(self):
        for e, ds in self.dsem.items():
            if ds.val > 0:
                self.prog[e].append(([(ds, ds.val)], None, None, 0))
        target = self.bar.val + len(ENGS)
        for e in ENGS:
            self.prog[e].append(([], "BARINC", self.bar, 1))
            self.prog[e].append(([(self.bar, target)], None, None, 0))
        self.bar.val = target
        snap = {}
        for s in self.all_sems():
            snap[s.name] = s.val
        for e in ENGS:
            self.wm[e] = dict(snap)

    def run(self, engines):
        for ename, eng in engines.items():
            for waits, fn, sem, n in self.prog[ename]:
                for s, v in waits:
                    eng.wait_ge(s.h, v)
                if fn is None:
                    continue
                if fn == "BARINC":
                    eng.nop().then_inc(sem.h, n)
                    continue
                ins = fn(eng)
                if sem is not None:
                    ins.then_inc(sem.h, n)


def build_program():
    nc = bass.Bass("TRN2", target_bir_lowering=False)

    xin = nc.dram_tensor("x", [NSEQ * T, D], F32, kind="ExternalInput").ap()
    wqkv = nc.dram_tensor("wqkv", [D, 3 * D], F32, kind="ExternalInput").ap()
    bqkv = nc.dram_tensor("bqkv", [3 * D], F32, kind="ExternalInput").ap()
    w1p = nc.dram_tensor("w1p", [D, 64], F32, kind="ExternalInput").ap()
    w2p = nc.dram_tensor("w2p", [D, 64], F32, kind="ExternalInput").ap()
    pscale = nc.dram_tensor("pscale", [H], F32, kind="ExternalInput").ap()
    woutw = nc.dram_tensor("wout", [D, D], F32, kind="ExternalInput").ap()
    boutw = nc.dram_tensor("bout", [D], F32, kind="ExternalInput").ap()
    gate = nc.dram_tensor("gate", [E, D], F32, kind="ExternalInput").ap()
    eup = nc.dram_tensor("eup", [E, D, II], F32, kind="ExternalInput").ap()
    edn = nc.dram_tensor("edn", [E, II, D], F32, kind="ExternalInput").ap()
    shupw = nc.dram_tensor("shup", [D, IS], F32, kind="ExternalInput").ap()
    shdnw = nc.dram_tensor("shdn", [IS, D], F32, kind="ExternalInput").ap()
    yout = nc.dram_tensor("y", [NSEQ * T, D], F32, kind="ExternalOutput").ap()
    h1sp = nc.dram_tensor("h1sp", [T, D], F32, kind="Internal").ap()
    dbg = {}
    if DEBUG:
        for nm, shp, dt in (
                ("d_xt", [128, 16384], BF16), ("d_linq", [128, 12288], BF16),
                ("d_link", [128, 12288], BF16), ("d_qt", [128, 16384], BF16),
                ("d_kt", [128, 16384], BF16), ("d_v", [128, 16384], BF16),
                ("d_aotq0", [128, 4096], BF16), ("d_aotq1", [128, 4096], BF16),
                ("d_h1", [T, D], F32), ("d_sct", [16, 2048], F32),
                ("d_combt", [16, 2048], BF16), ("d_moe1", [128, 16384], F32),
                ("d_h1t", [128, 16384], BF16), ("d_wk", [128, 768], F32)):
            dbg[nm] = nc.dram_tensor(nm, shp, dt, kind="ExternalOutput").ap()

    K = KB()

    from contextlib import ExitStack
    es = ExitStack()
    with es:
        AR_h = es.enter_context(nc.sbuf_tensor("AR", [128, ARENA_N], F32))
        IDB_h = es.enter_context(nc.sbuf_tensor("IDB", [128, 128], BF16))
        ID32_h = es.enter_context(nc.sbuf_tensor("ID32", [128, 128], F32))
        IOTA_h = es.enter_context(nc.sbuf_tensor("IOTA", [128, 128], F32))
        ME_h = es.enter_context(nc.sbuf_tensor("ME", [128, 16], F32))
        ME2_h = es.enter_context(nc.sbuf_tensor("ME2", [128, 16], F32))
        ONESB_h = es.enter_context(nc.sbuf_tensor("ONESB", [1, 128], BF16))
        ONES32_h = es.enter_context(nc.sbuf_tensor("ONES32", [1, 128], F32))
        BQKS_h = es.enter_context(nc.sbuf_tensor("BQKS", [128, 16], F32))
        PSCROW_h = es.enter_context(nc.sbuf_tensor("PSCROW", [1, 16], F32))
        PSCB_h = es.enter_context(nc.sbuf_tensor("PSCB", [128, 16], F32))
        PSC6_h = es.enter_context(nc.sbuf_tensor("PSC6", [128, 96], F32))
        GATET_h = es.enter_context(nc.sbuf_tensor("GATET", [128, 128], F32))
        BVBO_h = es.enter_context(nc.sbuf_tensor("BVBO", [1, 2048], BF16))
        NRH_h = es.enter_context(nc.sbuf_tensor("NRH", [128, 8], F32))
        RSH_h = es.enter_context(nc.sbuf_tensor("RSH", [128, 8], F32))
        EPI_h = es.enter_context(nc.sbuf_tensor("EPI", [128, 24], F32))
        MV_h = es.enter_context(nc.sbuf_tensor("MV", [128, 8], F32))
        ST6_h = es.enter_context(nc.sbuf_tensor("ST6", [128, 12], F32))
        RT1_h = es.enter_context(nc.sbuf_tensor("RT1", [128, 4], F32))
        RSC_h = es.enter_context(nc.sbuf_tensor("RSC", [128, 80], F32))
        EPSB_h = es.enter_context(nc.sbuf_tensor("EPSB", [128, 1], F32))
        TRI_h = es.enter_context(nc.sbuf_tensor("TRI", [128, 128], F32))
        PS_h = es.enter_context(nc.psum_tensor("PS", [128, 4096], F32))
        h_pe = es.enter_context(nc.semaphore("s_pe"))
        h_ve = es.enter_context(nc.semaphore("s_ve"))
        h_ae = es.enter_context(nc.semaphore("s_ae"))
        h_gp = es.enter_context(nc.semaphore("s_gp"))
        h_sy = es.enter_context(nc.semaphore("s_sy"))
        h_dgp = es.enter_context(nc.semaphore("s_dgp"))
        h_dsy = es.enter_context(nc.semaphore("s_dsy"))
        h_dae = es.enter_context(nc.semaphore("s_dae"))
        h_bar = es.enter_context(nc.semaphore("s_bar"))
        K.esem["pe"].h = h_pe
        K.esem["ve"].h = h_ve
        K.esem["ae"].h = h_ae
        K.esem["gp"].h = h_gp
        K.esem["sy"].h = h_sy
        K.dsem["gp"].h = h_dgp
        K.dsem["sy"].h = h_dsy
        K.dsem["ae"].h = h_dae
        K.bar.h = h_bar

        AR = AR_h.ap()
        IDB = IDB_h.ap()
        ID32 = ID32_h.ap()
        IOTA = IOTA_h.ap()
        ME = ME_h.ap()
        ME2 = ME2_h.ap()
        ONESB = ONESB_h.ap()
        ONES32 = ONES32_h.ap()
        BQKS = BQKS_h.ap()
        PSCROW = PSCROW_h.ap()
        PSCB = PSCB_h.ap()
        PSC6 = PSC6_h.ap()
        GATET = GATET_h.ap()
        BVBO = BVBO_h.ap()
        NRH = NRH_h.ap()
        RSH = RSH_h.ap()
        EPI = EPI_h.ap()
        MV = MV_h.ap()
        ST6 = ST6_h.ap()
        RT1 = RT1_h.ap()
        RSC = RSC_h.ap()
        EPSB = EPSB_h.ap()
        bEPSB = Buf()
        TRI = TRI_h.ap()
        bTRI = Buf()
        PS = PS_h.ap()

        def a32(off, n):
            return AR[:, off:off + n]

        def a16(off, n32):
            return AR[:, off:off + n32].bitcast(BF16)

        def bk(i, n=1):
            return PS[:, i * 512:(i + n) * 512]

        def bkb(i, n=1):
            return PS[:, i * 512:(i + n) * 512].bitcast(BF16)

        op = K.op

        bIDB = Buf(); bID32 = Buf(); bIOTA = Buf(); bME = Buf(); bME2 = Buf()
        bONESB = Buf(); bONES32 = Buf(); bBQKS = Buf(); bPSCROW = Buf()
        bPSCB = Buf(); bPSC6 = Buf(); bGATET = Buf(); bBVBO = Buf()
        bNRH = Buf(); bRSH = Buf(); bEPI = Buf(); bMVb = Buf(); bST6 = Buf()
        bRT1 = Buf(); bRSC = Buf()
        bH1SP = Buf()
        psb = [Buf() for _ in range(8)]

        op("gp", lambda g: g.iota(IOTA, pattern=[[1, 128]], base=0,
                                  channel_multiplier=-1,
                                  allow_small_or_imprecise_dtypes=True),
           writes=[bIOTA])
        op("gp", lambda g: g.iota(ME, pattern=[[128, 16]], base=1,
                                  channel_multiplier=1,
                                  allow_small_or_imprecise_dtypes=True),
           writes=[bME])
        op("gp", lambda g: g.iota(ME2, pattern=[[128, 16]], base=1 - 1024,
                                  channel_multiplier=1,
                                  allow_small_or_imprecise_dtypes=True),
           writes=[bME2])
        op("gp", lambda g: g.memset(ONESB, 1.0), writes=[bONESB])
        op("gp", lambda g: g.memset(EPSB, 1e-5), writes=[bEPSB])
        op("gp", lambda g: g.memset(ONES32, 1.0), writes=[bONES32])
        op("ve", lambda v: v.tensor_scalar(out=IDB, in0=IOTA, scalar1=0.0,
                                           scalar2=None, op0=AL.is_equal),
           reads=[bIOTA], writes=[bIDB])
        op("ve", lambda v: v.tensor_scalar(out=ID32, in0=IOTA, scalar1=0.0,
                                           scalar2=None, op0=AL.is_equal),
           reads=[bIOTA], writes=[bID32])
        op("ve", lambda v: v.tensor_scalar(out=TRI, in0=IOTA, scalar1=0.0,
                                           scalar2=-3.0e38, op0=AL.is_gt,
                                           op1=AL.mult),
           reads=[bIOTA], writes=[bTRI])
        op("sy", lambda s: s.dma_start(
            out=BQKS, in_=bqkv[0:2048].rearrange("(j p) -> p j", p=128),
            allow_slow_non_contiguous=True),
           writes=[bBQKS], dma=True)
        op("ve", lambda v: v.tensor_scalar_mul(BQKS[:, 0:8], BQKS[:, 0:8], 0.125),
           writes=[bBQKS])
        op("sy", lambda s: s.dma_start(out=PSCROW, in_=pscale[None, :]),
           writes=[bPSCROW], dma=True)
        op("gp", lambda g: g.dma_start(out=BVBO[:, 0:1024],
                                       in_=bqkv[None, 2048:3072]),
           writes=[bBVBO], dma=True)
        op("gp", lambda g: g.dma_start(out=BVBO[:, 1024:2048],
                                       in_=boutw[None, :]),
           writes=[bBVBO], dma=True)
        op("pe", lambda t: t.matmul(bk(7)[:, 0:16], ONES32, PSCROW,
                                    start=True, stop=True),
           reads=[bONES32, bPSCROW], writes=[psb[7]])
        op("ve", lambda v: v.tensor_copy(PSCB, bk(7)[:, 0:16]),
           reads=[psb[7]], writes=[bPSCB])
        op("ve", lambda v: v.tensor_copy(
            PSC6.rearrange("p (h c) -> p h c", h=16),
            PSCB[:, :, None].broadcast_to([128, 16, 6])),
           reads=[bPSCB], writes=[bPSC6])
        GST = a32(45056, 1024)
        bGST = Buf()
        op("sy", lambda s: s.dma_start(out=GST[0:16, :], in_=gate),
           writes=[bGST], dma=True)
        for c in range(8):
            b = c % 2
            op("pe", lambda t, c=c, b=b: t.transpose(
                bk(b)[:, 0:16], GST[0:16, c * 128:(c + 1) * 128],
                ID32[0:16, 0:16]),
               reads=[bGST, bID32], writes=[psb[b]])
            op("ve", lambda v, c=c, b=b: v.tensor_copy(
                GATET[:, c * 16:(c + 1) * 16], bk(b)[:, 0:16]),
               reads=[psb[b]], writes=[bGATET])

        K.barrier()

        nstore = [0]

        def tap(name, ap_, rbufs, seq_gate):
            if DEBUG and seq_gate:
                op("sy", lambda s_, ap_=ap_, name=name: s_.dma_start(
                    out=dbg[name], in_=ap_), reads=list(rbufs), dma=True)

        for s in range(NSEQ):
            xs = xin[s * T:(s + 1) * T, :]
            ys = yout[s * T:(s + 1) * T, :]

            XT = a16(0, 8192)
            bXT = Buf()
            xtok = [a16(45056, 2048), a16(47104, 2048)]
            bxtok = [Buf(), Buf()]
            for tc in range(4):
                sl = tc % 2
                op("gp", lambda g, tc=tc, sl=sl, xs=xs: g.dma_start(
                    out=xtok[sl].rearrange("p (t d) -> p t d", t=4),
                    in_=xs[tc * 512:(tc + 1) * 512, :]
                    .rearrange("(t p) d -> p t d", p=128)),
                   writes=[bxtok[sl]], dma=True)
                for c in range(8):
                    b = c % 2
                    for t4 in range(4):
                        op("pe", lambda t, c=c, t4=t4, b=b, sl=sl: t.transpose(
                            bkb(b)[:, t4 * 128:(t4 + 1) * 128],
                            xtok[sl][:, (t4 * 8 + c) * 128:(t4 * 8 + c + 1) * 128],
                            IDB),
                           reads=[bxtok[sl], bIDB], writes=[psb[b]])
                    op("ve", lambda v, c=c, tc=tc, b=b: v.tensor_copy(
                        XT[:, c * 2048 + tc * 512:c * 2048 + tc * 512 + 512],
                        bkb(b)[:, 0:512]),
                       reads=[psb[b]], writes=[bXT])

            K.barrier()

            LTQ = [a16(45056, 288), a16(45344, 288)]
            LTK = [a16(45632, 288), a16(45920, 288)]
            P12A = a32(46208, 2048)
            WKA = a32(48256, 96)
            WKB = a32(48352, 96)
            WKC = a32(48448, 96)
            WKD = a32(48544, 96)
            WKR = a32(48640, 96)
            WKN = a32(48736, 96)
            WSM = a32(48832, 48)
            WPB = a16(48880, 512)
            LINQ = a16(32768, 6144)
            LINK = a16(38912, 6144)
            bLTQ = [Buf(), Buf()]
            bLTK = [Buf(), Buf()]
            bP12A = Buf()
            bWKA = Buf(); bWKB = Buf(); bWKC = Buf(); bWKD = Buf()
            bWKR = Buf(); bWKN = Buf(); bWSM = Buf(); bWPB = Buf()
            bLINQ = Buf(); bLINK = Buf()

            for sl in range(2):
                op("ve", lambda v, sl=sl: v.memset(LTQ[sl], 0.0),
                   writes=[bLTQ[sl]])
                op("ve", lambda v, sl=sl: v.memset(LTK[sl], 0.0),
                   writes=[bLTK[sl]])
            op("gp", lambda g: g.memset(LINQ, 0.0), writes=[bLINQ])
            op("gp", lambda g: g.memset(LINK, 0.0), writes=[bLINK])
            op("gp", lambda g: g.dma_start(
                out=WPB.rearrange("p (c m) -> p c m", c=8)[:, :, 0:64],
                in_=w1p.rearrange("(c k) m -> k c m", c=8)),
               writes=[bWPB], dma=True)
            op("gp", lambda g: g.dma_start(
                out=WPB.rearrange("p (c m) -> p c m", c=8)[:, :, 64:128],
                in_=w2p.rearrange("(c k) m -> k c m", c=8)),
               writes=[bWPB], dma=True)

            for tt in range(NT):
                b = tt % 2
                for c in range(8):
                    op("pe", lambda t, c=c, tt=tt, b=b: t.matmul(
                        bk(b)[:, 0:128],
                        XT[:, c * 2048 + tt * 128:c * 2048 + tt * 128 + 128],
                        WPB[:, c * 128:(c + 1) * 128],
                        start=(c == 0), stop=(c == 7)),
                       reads=[bXT, bWPB], writes=[psb[b]])
                op("ve", lambda v, tt=tt, b=b: v.tensor_copy(
                    P12A[:, tt * 128:(tt + 1) * 128], bk(b)[:, 0:128]),
                   reads=[psb[b]], writes=[bP12A])

            def emit_lines_math(tt):
                sl = tt % 2
                P12 = P12A[:, tt * 128:(tt + 1) * 128]
                p1 = P12[:, 0:64].rearrange("p (h c) -> p h c", h=16)
                p2 = P12[:, 64:128].rearrange("p (h c) -> p h c", h=16)
                vA = WKA.rearrange("p (h c) -> p h c", h=16)
                vB = WKB.rearrange("p (h c) -> p h c", h=16)
                vC = WKC.rearrange("p (h c) -> p h c", h=16)
                vD = WKD.rearrange("p (h c) -> p h c", h=16)
                for (dst, bdst, srcp) in ((vA, bWKA, p1), (vD, bWKD, p2)):
                    op("ve", lambda v, dst=dst, srcp=srcp: v.tensor_copy(
                        dst[:, :, 0:3],
                        srcp[:, :, 0:1].broadcast_to([128, 16, 3])),
                       reads=[bP12A], writes=[bdst])
                    op("ve", lambda v, dst=dst, srcp=srcp: v.tensor_copy(
                        dst[:, :, 3:5],
                        srcp[:, :, 1:2].broadcast_to([128, 16, 2])),
                       reads=[bP12A], writes=[bdst])
                    op("ve", lambda v, dst=dst, srcp=srcp: v.tensor_copy(
                        dst[:, :, 5:6], srcp[:, :, 2:3]),
                       reads=[bP12A], writes=[bdst])
                for (dst, bdst, srcp) in ((vB, bWKB, p2), (vC, bWKC, p1)):
                    op("ve", lambda v, dst=dst, srcp=srcp: v.tensor_copy(
                        dst[:, :, 0:3], srcp[:, :, 1:4]),
                       reads=[bP12A], writes=[bdst])
                    op("ve", lambda v, dst=dst, srcp=srcp: v.tensor_copy(
                        dst[:, :, 3:5], srcp[:, :, 2:4]),
                       reads=[bP12A], writes=[bdst])
                    op("ve", lambda v, dst=dst, srcp=srcp: v.tensor_copy(
                        dst[:, :, 5:6], srcp[:, :, 3:4]),
                       reads=[bP12A], writes=[bdst])
                op("ve", lambda v: v.tensor_mul(WKR, WKA, WKB),
                   reads=[bWKA, bWKB], writes=[bWKR])
                op("ve", lambda v: v.tensor_mul(WKA, WKC, WKD),
                   reads=[bWKC, bWKD], writes=[bWKA])
                op("ve", lambda v: v.tensor_sub(WKR, WKR, WKA),
                   reads=[bWKA], writes=[bWKR])
                op("ve", lambda v: v.tensor_mul(WKA, WKR, WKR),
                   reads=[bWKR], writes=[bWKA])
                op("ve", lambda v: v.reduce_sum(
                    WSM[:, 0:16], WKA.rearrange("p (h c) -> p h c", h=16),
                    axis=AX), reads=[bWKA], writes=[bWSM])
                op("ae", lambda a: a.activation(WSM[:, 16:32], WSM[:, 0:16],
                                                AF.Sqrt),
                   reads=[bWSM], writes=[bWSM])
                op("ve", lambda v: v.tensor_scalar_max(
                    WSM[:, 16:32], WSM[:, 16:32], 1e-12), writes=[bWSM])
                op("ve", lambda v: v.reciprocal(WSM[:, 32:48], WSM[:, 16:32]),
                   writes=[bWSM])
                op("ve", lambda v: v.tensor_mul(
                    WKN.rearrange("p (h c) -> p h c", h=16),
                    WKR.rearrange("p (h c) -> p h c", h=16),
                    WSM[:, 32:48][:, :, None].broadcast_to([128, 16, 6])),
                   reads=[bWKR, bWSM], writes=[bWKN])
                op("ve", lambda v: v.tensor_mul(WKR, WKN, PSC6),
                   reads=[bWKN, bPSC6], writes=[bWKR])
                if DEBUG and s == 0 and tt == 0:
                    op("sy", lambda s_: s_.dma_start(
                        out=dbg["d_wk"][:, 0:96], in_=WKA),
                       reads=[bWKA], dma=True)
                    op("sy", lambda s_: s_.dma_start(
                        out=dbg["d_wk"][:, 96:192], in_=WKB),
                       reads=[bWKB], dma=True)
                    op("sy", lambda s_: s_.dma_start(
                        out=dbg["d_wk"][:, 192:288], in_=WKC),
                       reads=[bWKC], dma=True)
                    op("sy", lambda s_: s_.dma_start(
                        out=dbg["d_wk"][:, 288:384], in_=WKD),
                       reads=[bWKD], dma=True)
                    op("sy", lambda s_: s_.dma_start(
                        out=dbg["d_wk"][:, 384:480], in_=WKN),
                       reads=[bWKN], dma=True)
                    op("sy", lambda s_: s_.dma_start(
                        out=dbg["d_wk"][:, 480:576], in_=WKR),
                       reads=[bWKR], dma=True)
                    op("sy", lambda s_: s_.dma_start(
                        out=dbg["d_wk"][:, 576:624], in_=WSM[:, 0:48]),
                       reads=[bWSM], dma=True)
                    op("sy", lambda s_: s_.dma_start(
                        out=dbg["d_wk"][:, 624:720], in_=PSC6),
                       reads=[bPSC6], dma=True)
                vN = WKN.rearrange("p (h c) -> p h c", h=16)
                vR = WKR.rearrange("p (h c) -> p h c", h=16)
                ltq = LTQ[sl].rearrange("p (g a w) -> p g a w", g=6, a=3)
                ltk = LTK[sl].rearrange("p (g a w) -> p g a w", g=6, a=3)
                op("ve", lambda v, ltq=ltq, vR=vR, sl=sl: v.tensor_copy(
                    ltq[:, 0:5, :, 0:6],
                    vR[:, 0:15, :].rearrange("p (g a) c -> p g a c", g=5)),
                   reads=[bWKR], writes=[bLTQ[sl]])
                op("ve", lambda v, ltq=ltq, vR=vR, sl=sl: v.tensor_copy(
                    ltq[:, 5:6, 0:1, 0:6], vR[:, 15:16, None, :]),
                   reads=[bWKR], writes=[bLTQ[sl]])
                op("ve", lambda v, ltk=ltk, vN=vN, sl=sl: v.tensor_copy(
                    ltk[:, 0:5, :, 0:6],
                    vN[:, 0:15, ::-1].rearrange("p (g a) c -> p g a c", g=5)),
                   reads=[bWKN], writes=[bLTK[sl]])
                op("ve", lambda v, ltk=ltk, vN=vN, sl=sl: v.tensor_copy(
                    ltk[:, 5:6, 0:1, 0:6], vN[:, 15:16, None, ::-1]),
                   reads=[bWKN], writes=[bLTK[sl]])
                op("ve", lambda v, ltk=ltk, sl=sl: v.tensor_scalar_mul(
                    ltk[:, :, :, 1:2], ltk[:, :, :, 1:2], -1.0),
                   writes=[bLTK[sl]])
                op("ve", lambda v, ltk=ltk, sl=sl: v.tensor_scalar_mul(
                    ltk[:, :, :, 4:5], ltk[:, :, :, 4:5], -1.0),
                   writes=[bLTK[sl]])

            def emit_lines_T(tt):
                sl = tt % 2
                for g in range(6):
                    for (srcb, bsrc, dst, bdst) in (
                            (LTQ[sl], bLTQ[sl], LINQ, bLINQ),
                            (LTK[sl], bLTK[sl], LINK, bLINK)):
                        b = 2 + (g % 2)
                        op("pe", lambda t, g=g, srcb=srcb, b=b, bsrc=bsrc:
                           t.transpose(
                               bkb(b)[0:96, 0:128],
                               srcb[:, g * 96:(g + 1) * 96], IDB),
                           reads=[bsrc, bIDB], writes=[psb[b]])
                        op("ve", lambda v, g=g, tt=tt, dst=dst, b=b:
                           v.tensor_copy(
                               dst[0:96, g * 2048 + tt * 128:
                                   g * 2048 + tt * 128 + 128],
                               bkb(b)[0:96, 0:128]),
                           reads=[psb[b]], writes=[bdst])

            for tt in range(NT):
                emit_lines_math(tt)
                if tt > 0:
                    emit_lines_T(tt - 1)
            emit_lines_T(NT - 1)
            tap("d_xt", XT, [bXT], s == 0)
            tap("d_linq", LINQ, [bLINQ], s == 0)
            tap("d_link", LINK, [bLINK], s == 0)

            K.barrier()

            QT = a16(8192, 8192)
            KT = a16(16384, 8192)
            bQT = Buf(); bKT = Buf()
            WQK = [a16(45056, 512), a16(45568, 512)]
            bWQK = [Buf(), Buf()]
            for pb in range(16):
                sl = pb % 2
                op("gp", lambda g, pb=pb, sl=sl: g.dma_start(
                    out=WQK[sl].rearrange("p (c m) -> p c m", c=8),
                    in_=wqkv[:, pb * 128:(pb + 1) * 128]
                    .rearrange("(c k) m -> k c m", c=8)),
                   writes=[bWQK[sl]], dma=True)
                dst = QT if pb < 8 else KT
                bdst = bQT if pb < 8 else bKT
                scale = 0.125 if pb < 8 else 1.0
                for tcb in range(4):
                    b = 4 + (tcb % 2)
                    for c in range(8):
                        op("pe", lambda t, pb=pb, sl=sl, tcb=tcb, c=c, b=b:
                           t.matmul(
                               bk(b), WQK[sl][:, c * 128:(c + 1) * 128],
                               XT[:, c * 2048 + tcb * 512:
                                  c * 2048 + tcb * 512 + 512],
                               start=(c == 0), stop=(c == 7)),
                           reads=[bWQK[sl], bXT], writes=[psb[b]])
                    op("ae", lambda a, pb=pb, tcb=tcb, b=b, dst=dst,
                       scale=scale: a.activation(
                           dst[:, (pb % 8) * 2048 + tcb * 512:
                               (pb % 8) * 2048 + tcb * 512 + 512],
                           bk(b), AF.Identity, bias=BQKS[:, pb:pb + 1],
                           scale=scale),
                       reads=[psb[b], bBQKS], writes=[bdst])

            V = a16(24576, 8192)
            bV = Buf()
            WV = a16(45056, 4096)
            bWV = Buf(reuse=bWQK)
            op("gp", lambda g: g.dma_start(
                out=WV.rearrange("p (c m) -> p c m", c=8),
                in_=wqkv[:, 2048:3072].rearrange("(c k) m -> k c m", c=8)),
               writes=[bWV], dma=True)
            for tt in range(NT):
                b0 = 0 if tt % 2 == 0 else 2
                for c in range(8):
                    for hf in range(2):
                        op("pe", lambda t, tt=tt, c=c, hf=hf, b0=b0: t.matmul(
                            bk(b0 + hf),
                            XT[:, c * 2048 + tt * 128:c * 2048 + tt * 128 + 128],
                            WV[:, c * 1024 + hf * 512:c * 1024 + hf * 512 + 512],
                            start=(c == 0), stop=False),
                           reads=[bXT, bWV], writes=[psb[b0 + hf]])
                for hf in range(2):
                    op("pe", lambda t, hf=hf, b0=b0: t.matmul(
                        bk(b0 + hf), ONESB,
                        BVBO[:, hf * 512:(hf + 1) * 512],
                        start=False, stop=True),
                       reads=[bONESB, bBVBO], writes=[psb[b0 + hf]])
                op("ve", lambda v, tt=tt, b0=b0: v.tensor_copy(
                    V[:, tt * 1024:(tt + 1) * 1024], bk(b0, 2)),
                   reads=[psb[b0], psb[b0 + 1]], writes=[bV])
            tap("d_qt", QT, [bQT], s == 0)
            tap("d_kt", KT, [bKT], s == 0)
            tap("d_v", V, [bV], s == 0)

            K.barrier()

            PB = [a16(0, 1024), a16(1024, 1024)]
            bPB = [Buf(), Buf()]
            PT = a16(2048, 4096)
            bPT = Buf()
            AOTQ = a16(6144, 2048)
            bAOTQ = Buf()
            WOUT = a16(45056, 4096)
            bWOUT = Buf(reuse=[bWV])
            XT32 = a32(49152, 1024)
            bXT32 = Buf()
            SH = [(0, psb[0], psb[1]), (2, psb[2], psb[3])]

            for qb in range(4):
                W = 512 * (qb + 1)
                if W <= 1024:
                    halves = [(0, W)]
                else:
                    halves = [(0, 1024), (1024, W - 1024)]
                nh = len(halves)
                nkc = W // 128

                op("gp", lambda g, qb=qb: g.dma_start(
                    out=WOUT.rearrange("p (c m) -> p c m", c=8),
                    in_=woutw.rearrange("(c k) m -> k c m", c=8)),
                   writes=[bWOUT], dma=True)

                def emit_S(h, qt, qb=qb, nh=nh):
                    par = h % 2
                    pair = h // 2
                    g6 = h // 3
                    a3 = h % 3
                    qg = qb * 4 + qt
                    Wv = 128 * qg + 128
                    if Wv <= 1024:
                        halves_ex = [(0, Wv)]
                    else:
                        halves_ex = [(0, 1024), (1024, Wv - 1024)]
                    assert len(halves_ex) == nh
                    for hi, (koff, wd) in enumerate(halves_ex):
                        shi = (qt % 2) if nh == 1 else hi
                        sb0 = SH[shi][0]
                        col = qt * 2 + hi
                        nkc_h = (wd + 511) // 512
                        cbufs = [SH[shi][1 + kc] for kc in range(nkc_h)]
                        for kc in range(nkc_h):
                            cw = min(512, wd - kc * 512)
                            op("pe", lambda t, par=par, pair=pair, qg=qg,
                               koff=koff, kc=kc, sb0=sb0, cw=cw: t.matmul(
                                   bk(sb0 + kc)[:, 0:cw],
                                   QT[64 * par:64 * par + 64,
                                      pair * 2048 + qg * 128:
                                      pair * 2048 + qg * 128 + 128],
                                   KT[64 * par:64 * par + 64,
                                      pair * 2048 + koff + kc * 512:
                                      pair * 2048 + koff + kc * 512 + cw],
                                   start=True, stop=False),
                               reads=[bQT, bKT], writes=[cbufs[kc]])
                        for kc in range(nkc_h):
                            cw = min(512, wd - kc * 512)
                            op("pe", lambda t, g6=g6, a3=a3, qg=qg,
                               koff=koff, kc=kc, sb0=sb0, cw=cw: t.matmul(
                                   bk(sb0 + kc)[:, 0:cw],
                                   LINQ[32 * a3:32 * a3 + 8,
                                        g6 * 2048 + qg * 128:
                                        g6 * 2048 + qg * 128 + 128],
                                   LINK[32 * a3:32 * a3 + 8,
                                        g6 * 2048 + koff + kc * 512:
                                        g6 * 2048 + koff + kc * 512 + cw],
                                   start=False, stop=True),
                               reads=[bLINQ, bLINK], writes=[cbufs[kc]])
                        if koff + wd == Wv:
                            op("ve", lambda v, sb0=sb0, wd=wd:
                               v.tensor_add(
                                   bk(sb0, 2)[:, wd - 128:wd],
                                   bk(sb0, 2)[:, wd - 128:wd], TRI),
                               reads=[bTRI, cbufs[-1]], writes=[cbufs[-1]])
                        op("ve", lambda v, sb0=sb0, wd=wd, col=col:
                           v.reduce_max(
                               NRH[:, col:col + 1],
                               bk(sb0, 2)[:, 0:wd], axis=AX, negate=True),
                           reads=cbufs, writes=[bNRH])
                        op("ae", lambda a, sb0=sb0, wd=wd, koff=koff,
                           qt=qt, col=col: a.activation(
                               PB[qt % 2][:, koff:koff + wd],
                               bk(sb0, 2)[:, 0:wd], AF.Exp,
                               bias=NRH[:, col:col + 1], scale=1.0,
                               accum_out=RSH[:, col:col + 1]),
                           reads=cbufs + [bNRH],
                           writes=[bRSH, bPB[qt % 2]])

                def emit_epi(h, qt, nh=nh):
                    p0 = qt - 1
                    c0 = p0 * 2
                    if nh == 1:
                        op("ve", lambda v, c0=c0: v.reciprocal(
                            EPI[:, 20:24].rearrange(
                                "p (q i) -> p q i", q=2)[:, :, 0:1],
                            RSH[:, c0:c0 + 4].rearrange(
                                "p (q i) -> p q i", q=2)[:, :, 0:1]),
                           reads=[bRSH], writes=[bEPI])
                    else:
                        nr4 = NRH[:, c0:c0 + 4].rearrange(
                            "p (q i) -> p q i", q=2)
                        rs4 = RSH[:, c0:c0 + 4].rearrange(
                            "p (q i) -> p q i", q=2)
                        e4 = EPI[:, 4:8].rearrange("p (q i) -> p q i", q=2)
                        e8 = EPI[:, 8:12].rearrange("p (q i) -> p q i", q=2)
                        e12 = EPI[:, 12:16].rearrange("p (q i) -> p q i", q=2)
                        e20 = EPI[:, 20:24].rearrange("p (q i) -> p q i", q=2)
                        op("ve", lambda v, nr4=nr4: v.tensor_tensor(
                            EPI[:, 0:2][:, :, None],
                            nr4[:, :, 0:1], nr4[:, :, 1:2], AL.min),
                           reads=[bNRH], writes=[bEPI])
                        op("ve", lambda v, nr4=nr4, e4=e4: v.tensor_sub(
                            e4, EPI[:, 0:2][:, :, None]
                            .broadcast_to([128, 2, 2]), nr4),
                           reads=[bNRH], writes=[bEPI])
                        op("ae", lambda a: a.activation(
                            EPI[:, 8:12], EPI[:, 4:8], AF.Exp),
                           reads=[bEPI], writes=[bEPI])
                        op("ve", lambda v, rs4=rs4, e8=e8, e12=e12:
                           v.tensor_mul(e12, rs4, e8),
                           reads=[bRSH, bEPI], writes=[bEPI])
                        op("ve", lambda v, e12=e12: v.tensor_add(
                            EPI[:, 16:18][:, :, None],
                            e12[:, :, 0:1], e12[:, :, 1:2]),
                           reads=[bEPI], writes=[bEPI])
                        op("ve", lambda v: v.reciprocal(
                            EPI[:, 18:20], EPI[:, 16:18]),
                           reads=[bEPI], writes=[bEPI])
                        op("ve", lambda v, e8=e8, e20=e20: v.tensor_mul(
                            e20, e8, EPI[:, 18:20][:, :, None]
                            .broadcast_to([128, 2, 2])),
                           reads=[bEPI], writes=[bEPI])
                    for qt2 in (p0, qt):
                        qg2 = qb * 4 + qt2
                        Wv2 = 128 * qg2 + 128
                        if Wv2 <= 1024:
                            hv = [(0, Wv2)]
                        else:
                            hv = [(0, 1024), (1024, Wv2 - 1024)]
                        for hi, (koff, wd) in enumerate(hv):
                            sc_col = 20 + (qt2 - p0) * 2 + hi
                            op("ve", lambda v, qt2=qt2, koff=koff,
                               wd=wd, sc_col=sc_col: v.tensor_scalar_mul(
                                   PB[qt2 % 2][:, koff:koff + wd],
                                   PB[qt2 % 2][:, koff:koff + wd],
                                   EPI[:, sc_col:sc_col + 1]),
                               reads=[bEPI], writes=[bPB[qt2 % 2]])

                def emit_T(h, qt, qb=qb, W=W):
                    qg = qb * 4 + qt
                    nke = qg + 1
                    tb = 4 + (qt % 2)
                    for kg in range((nke + 3) // 4):
                        kcs = list(range(kg * 4, min(kg * 4 + 4, nke)))
                        for j, kc in enumerate(kcs):
                            op("pe", lambda t, qt=qt, kc=kc, j=j, tb=tb:
                               t.transpose(
                                   bkb(tb)[:, j * 128:(j + 1) * 128],
                                   PB[qt % 2][:, kc * 128:(kc + 1) * 128],
                                   IDB),
                               reads=[bPB[qt % 2], bIDB], writes=[psb[tb]])
                        eng2 = "ve" if (kg + qt) % 2 == 0 else "ae"
                        dst = PT.rearrange("p (k q) -> p k q", q=512)[
                            :, kg * 4:kg * 4 + len(kcs),
                            qt * 128:qt * 128 + 128]
                        srcp = bkb(tb)[:, 0:len(kcs) * 128].rearrange(
                            "p (k q) -> p k q", q=128)
                        if eng2 == "ve":
                            op("ve", lambda v, dst=dst, srcp=srcp:
                               v.tensor_copy(dst, srcp),
                               reads=[psb[tb]], writes=[bPT])
                        else:
                            op("ae", lambda a, dst=dst, srcp=srcp:
                               a.copy(out=dst, in_=srcp),
                               reads=[psb[tb]], writes=[bPT])
                    if nke < W // 128:
                        slv = PT.rearrange("p (k q) -> p k q", q=512)[
                            :, nke:W // 128, qt * 128:qt * 128 + 128]
                        op("ve", lambda v, slv=slv: v.memset(slv, 0.0),
                           writes=[bPT])

                def emit_AV(h, W=W):
                    par = h % 2
                    pair = h // 2
                    ob = 6 + (pair % 2)
                    nkc = W // 128
                    for kc in range(nkc):
                        op("pe", lambda t, h=h, kc=kc, par=par, ob=ob,
                           nkc=nkc: t.matmul(
                               PS[64 * par:64 * par + 64,
                                  ob * 512:ob * 512 + 512],
                               V[:, kc * 1024 + h * 64:kc * 1024 + h * 64 + 64],
                               PT[:, kc * 512:(kc + 1) * 512],
                               start=(kc == 0), stop=(kc == nkc - 1)),
                           reads=[bV, bPT], writes=[psb[ob]])
                    if par == 1:
                        op("ve", lambda v, pair=pair, ob=ob: v.tensor_copy(
                            AOTQ[:, pair * 512:(pair + 1) * 512], bk(ob)),
                           reads=[psb[ob]], writes=[bAOTQ])

                for h in range(16):
                    if h > 0:
                        emit_T(h - 1, 2)
                        emit_T(h - 1, 3)
                    emit_S(h, 0)
                    emit_S(h, 1)
                    emit_epi(h, 1)
                    if h > 0:
                        emit_AV(h - 1)
                    emit_T(h, 0)
                    emit_T(h, 1)
                    emit_S(h, 2)
                    emit_S(h, 3)
                    emit_epi(h, 3)
                emit_T(15, 2)
                emit_T(15, 3)
                emit_AV(15)
                if qb == 0:
                    tap("d_aotq0", AOTQ, [bAOTQ], s == 0)
                if qb == 1:
                    tap("d_aotq1", AOTQ, [bAOTQ], s == 0)

                for ttl in range(4):
                    tt = qb * 4 + ttl
                    op("sy", lambda s_, tt=tt, xs=xs: s_.dma_start(
                        out=XT32,
                        in_=xs[tt * 128:(tt + 1) * 128, :]),
                       writes=[bXT32], dma=True)
                    for hf in range(2):
                        ob = 6 + hf
                        for pbk in range(8):
                            op("pe", lambda t, ttl=ttl, hf=hf, pbk=pbk, ob=ob:
                               t.matmul(
                                   bk(ob),
                                   AOTQ[:, pbk * 512 + ttl * 128:
                                        pbk * 512 + ttl * 128 + 128],
                                   WOUT[:, pbk * 1024 + hf * 512:
                                        pbk * 1024 + hf * 512 + 512],
                                   start=(pbk == 0), stop=False),
                               reads=[bAOTQ, bWOUT], writes=[psb[ob]])
                        op("pe", lambda t, hf=hf, ob=ob: t.matmul(
                            bk(ob), ONESB,
                            BVBO[:, 1024 + hf * 512:1024 + (hf + 1) * 512],
                            start=False, stop=True),
                           reads=[bONESB, bBVBO], writes=[psb[ob]])
                    op("ve", lambda v: v.tensor_add(XT32, XT32, bk(6, 2)),
                       reads=[psb[6], psb[7]], writes=[bXT32])
                    op("ve", lambda v: v.bn_stats(ST6[:, 0:6], XT32[:, 0:512]),
                       reads=[bXT32], writes=[bST6])
                    op("ve", lambda v: v.bn_stats(ST6[:, 6:12], XT32[:, 512:1024]),
                       reads=[bXT32], writes=[bST6])
                    op("ve", lambda v: v.bn_aggr(MV[:, 0:2], ST6),
                       reads=[bST6], writes=[bMVb])
                    op("ae", lambda a: a.activation(RT1[:, 0:1], MV[:, 1:2],
                                                    AF.Sqrt, bias=EPSB),
                       reads=[bMVb, bEPSB], writes=[bRT1])
                    op("ve", lambda v: v.reciprocal(RT1[:, 1:2], RT1[:, 0:1]),
                       writes=[bRT1])
                    op("ve", lambda v: v.scalar_tensor_tensor(
                        out=RT1[:, 2:3], in0=MV[:, 0:1], scalar=-1.0,
                        in1=RT1[:, 1:2], op0=AL.mult, op1=AL.mult),
                       reads=[bMVb], writes=[bRT1])
                    op("ae", lambda a: a.activation(
                        XT32, XT32, AF.Identity, bias=RT1[:, 2:3],
                        scale=RT1[:, 1:2]),
                       reads=[bRT1], writes=[bXT32])
                    op("ae", lambda a, tt=tt: a.dma_start(
                        out=h1sp[tt * 128:(tt + 1) * 128, :], in_=XT32),
                       reads=[bXT32], writes=[bH1SP], dma=True)
                    if DEBUG and s == 0:
                        op("ae", lambda a, tt=tt: a.dma_start(
                            out=dbg["d_h1"][tt * 128:(tt + 1) * 128, :],
                            in_=XT32), reads=[bXT32], dma=True)

            K.barrier()

            H1T = a16(0, 8192)
            bH1T = Buf(reuse=[bPB[0], bPB[1], bPT, bAOTQ])
            MOE = a32(8192, 16384)
            bMOE = Buf(reuse=[bQT, bKT])

            def moe_t(tt):
                return a32(8192 + tt * 1024, 1024)

            G32 = a32(24576, 4096)
            bG32 = Buf(reuse=[bV])
            H1F = a32(28672, 4096)
            bH1F = Buf(reuse=[bV])
            H1B = a16(36096, 2048)
            bH1B = Buf(reuse=[bLINQ])
            COMBT = a16(32768, 1024)
            bCOMBT = Buf(reuse=[bLINQ])
            SCT = a32(33792, 2048)
            bSCT = Buf(reuse=[bLINQ])
            SCTOK = a32(35840, 256)
            bSCTOK = Buf(reuse=[bLINQ])
            COMBW = a32(38144, 16)
            bCOMBW = Buf(reuse=[bLINQ])

            for tc in range(4):
                op("sy", lambda s_, tc=tc: s_.dma_start(
                    out=H1F.rearrange("p (t d) -> p t d", t=4),
                    in_=h1sp[tc * 512:(tc + 1) * 512, :]
                    .rearrange("(t p) d -> p t d", p=128)),
                   reads=[bH1SP], writes=[bH1F], dma=True)
                op("gp", lambda g, tc=tc: g.dma_start(
                    out=H1B.rearrange("p (t d) -> p t d", t=4),
                    in_=h1sp[tc * 512:(tc + 1) * 512, :]
                    .rearrange("(t p) d -> p t d", p=128)),
                   reads=[bH1SP], writes=[bH1B], dma=True)
                for t4 in range(4):
                    tt = tc * 4 + t4
                    op("ve", lambda v, t4=t4, tt=tt: v.tensor_copy(
                        moe_t(tt), H1F[:, t4 * 1024:(t4 + 1) * 1024]),
                       reads=[bH1F], writes=[bMOE])
                    for c in range(8):
                        op("pe", lambda t, t4=t4, c=c: t.transpose(
                            bkb(4)[:, c * 128:(c + 1) * 128],
                            H1B[:, t4 * 1024 + c * 128:t4 * 1024 + c * 128 + 128],
                            IDB),
                           reads=[bH1B, bIDB], writes=[psb[4]])
                    op("ve", lambda v, tt=tt: v.tensor_copy(
                        H1T.rearrange("p (c q) -> p c q", c=8)
                        [:, :, tt * 128:tt * 128 + 128],
                        bkb(4)[:, 0:1024].rearrange("p (c q) -> p c q", c=8)),
                       reads=[psb[4]], writes=[bH1T])
                    for c in range(8):
                        b = 5 if c < 4 else 6
                        op("pe", lambda t, t4=t4, c=c, b=b: t.transpose(
                            bk(b)[:, (c % 4) * 128:(c % 4) * 128 + 128],
                            H1F[:, t4 * 1024 + c * 128:t4 * 1024 + c * 128 + 128],
                            ID32),
                           reads=[bH1F, bID32], writes=[psb[b]])
                    op("ve", lambda v, t4=t4: v.tensor_copy(
                        G32.rearrange("p (c q) -> p c q", c=8)
                        [:, 0:4, t4 * 128:t4 * 128 + 128],
                        bk(5)[:, 0:512].rearrange("p (c q) -> p c q", c=4)),
                       reads=[psb[5]], writes=[bG32])
                    op("ve", lambda v, t4=t4: v.tensor_copy(
                        G32.rearrange("p (c q) -> p c q", c=8)
                        [:, 4:8, t4 * 128:t4 * 128 + 128],
                        bk(6)[:, 0:512].rearrange("p (c q) -> p c q", c=4)),
                       reads=[psb[6]], writes=[bG32])
                for c in range(8):
                    op("pe", lambda t, c=c: t.matmul(
                        PS[0:16, 7 * 512:7 * 512 + 512],
                        GATET[:, c * 16:(c + 1) * 16],
                        G32[:, c * 512:(c + 1) * 512],
                        start=(c == 0), stop=(c == 7)),
                       reads=[bGATET, bG32], writes=[psb[7]])
                op("ae", lambda a, tc=tc: a.activation(
                    SCT[0:16, tc * 512:(tc + 1) * 512],
                    PS[0:16, 7 * 512:7 * 512 + 512], AF.Tanh, scale=0.5),
                   reads=[psb[7]], writes=[bSCT])
                op("ve", lambda v, tc=tc: v.tensor_scalar(
                    out=SCT[0:16, tc * 512:(tc + 1) * 512],
                    in0=SCT[0:16, tc * 512:(tc + 1) * 512],
                    scalar1=0.5, scalar2=0.5, op0=AL.mult, op1=AL.add),
                   writes=[bSCT])
                for t4 in range(4):
                    tt = tc * 4 + t4
                    op("pe", lambda t, tc=tc, t4=t4: t.transpose(
                        bk(4)[:, 0:16],
                        SCT[0:16, (tc * 4 + t4) * 128:(tc * 4 + t4) * 128 + 128],
                        ID32[0:16, 0:16]),
                       reads=[bSCT, bID32], writes=[psb[4]])
                    op("ve", lambda v, tt=tt: v.tensor_copy(
                        SCTOK[:, tt * 16:(tt + 1) * 16], bk(4)[:, 0:16]),
                       reads=[psb[4]], writes=[bSCTOK])

            for tt in range(NT):
                sc3 = SCTOK[:, tt * 16:(tt + 1) * 16] \
                    .rearrange("p (g e) -> p g e", g=4)
                A6 = RSC[:, 0:24].rearrange("p (g c) -> p g c", g=4)
                B6 = RSC[:, 24:48].rearrange("p (g c) -> p g c", g=4)
                op("ve", lambda v, sc3=sc3, A6=A6: v.tensor_copy(
                    A6[:, :, 0:3], sc3[:, :, 0:1].broadcast_to([128, 4, 3])),
                   reads=[bSCTOK], writes=[bRSC])
                op("ve", lambda v, sc3=sc3, A6=A6: v.tensor_copy(
                    A6[:, :, 3:5], sc3[:, :, 1:2].broadcast_to([128, 4, 2])),
                   reads=[bSCTOK], writes=[bRSC])
                op("ve", lambda v, sc3=sc3, A6=A6: v.tensor_copy(
                    A6[:, :, 5:6], sc3[:, :, 2:3]),
                   reads=[bSCTOK], writes=[bRSC])
                op("ve", lambda v, sc3=sc3, B6=B6: v.tensor_copy(
                    B6[:, :, 0:3], sc3[:, :, 1:4]),
                   reads=[bSCTOK], writes=[bRSC])
                op("ve", lambda v, sc3=sc3, B6=B6: v.tensor_copy(
                    B6[:, :, 3:5], sc3[:, :, 2:4]),
                   reads=[bSCTOK], writes=[bRSC])
                op("ve", lambda v, sc3=sc3, B6=B6: v.tensor_copy(
                    B6[:, :, 5:6], sc3[:, :, 3:4]),
                   reads=[bSCTOK], writes=[bRSC])
                op("ve", lambda v, A6=A6, B6=B6: v.tensor_add(A6, A6, B6),
                   writes=[bRSC])
                op("ve", lambda v, A6=A6: v.reduce_max(
                    RSC[:, 48:52], A6, axis=AX), writes=[bRSC])
                op("ve", lambda v: v.memset(RSC[:, 52:60], -FMAXP),
                   writes=[bRSC])
                op("ve", lambda v: v.tensor_copy(RSC[:, 52:56], RSC[:, 48:52]),
                   writes=[bRSC])
                op("ve", lambda v: v.max(out=MV[:, 0:8], in_=RSC[:, 52:60]),
                   reads=[bRSC], writes=[bMVb])
                op("ve", lambda v: v.tensor_scalar(
                    out=RSC[:, 60:64], in0=RSC[:, 48:52],
                    scalar1=MV[:, 1:2], scalar2=None, op0=AL.is_ge),
                   reads=[bMVb], writes=[bRSC])
                op("ve", lambda v, sc3=sc3: v.tensor_mul(
                    RSC[:, 0:16].rearrange("p (g e) -> p g e", g=4), sc3,
                    RSC[:, 60:64][:, :, None].broadcast_to([128, 4, 4])),
                   reads=[bSCTOK], writes=[bRSC])
                op("ve", lambda v: v.max(out=MV[:, 0:8], in_=RSC[:, 0:16]),
                   reads=[bRSC], writes=[bMVb])
                op("ve", lambda v: v.tensor_scalar(
                    out=RSC[:, 16:32], in0=RSC[:, 0:16],
                    scalar1=MV[:, 3:4], scalar2=None, op0=AL.is_ge),
                   reads=[bMVb], writes=[bRSC])
                op("ve", lambda v: v.tensor_mul(
                    RSC[:, 32:48], RSC[:, 0:16], RSC[:, 16:32]),
                   writes=[bRSC])
                op("ve", lambda v: v.reduce_sum(
                    RSC[:, 64:65], RSC[:, 32:48], axis=AX), writes=[bRSC])
                op("ve", lambda v: v.tensor_scalar_add(
                    RSC[:, 64:65], RSC[:, 64:65], 1e-20), writes=[bRSC])
                op("ve", lambda v: v.reciprocal(RSC[:, 65:66], RSC[:, 64:65]),
                   writes=[bRSC])
                op("ve", lambda v: v.tensor_scalar_mul(
                    RSC[:, 65:66], RSC[:, 65:66], 2.5), writes=[bRSC])
                op("ve", lambda v: v.tensor_scalar_mul(
                    COMBW, RSC[:, 32:48], RSC[:, 65:66]),
                   reads=[bRSC], writes=[bCOMBW])
                op("pe", lambda t: t.transpose(bk(5)[0:16, 0:128], COMBW, ID32),
                   reads=[bCOMBW, bID32], writes=[psb[5]])
                op("ve", lambda v, tt=tt: v.tensor_copy(
                    COMBT[0:16, tt * 128:tt * 128 + 128],
                    bk(5)[0:16, 0:128]),
                   reads=[psb[5]], writes=[bCOMBT])

            K.barrier()

            tap("d_sct", SCT[0:16, :], [bSCT], s == 0)
            tap("d_combt", COMBT[0:16, :], [bCOMBT], s == 0)
            tap("d_h1t", H1T, [bH1T], s == 0)

            SHUP = [a16(24576, 2048), a16(26624, 2048)]
            bSHUP = [Buf(reuse=[bG32, bH1F]), Buf(reuse=[bG32, bH1F])]
            SHDN = [a16(28672, 2048), a16(30720, 2048)]
            bSHDN = [Buf(reuse=[bG32, bH1F]), Buf(reuse=[bG32, bH1F])]
            SHS = a16(38912, 4096)
            bSHS = Buf(reuse=[bLINK])

            for iq in range(4):
                sl = iq % 2
                op("gp", lambda g, iq=iq, sl=sl: g.dma_start(
                    out=SHUP[sl].rearrange("p (c m) -> p c m", c=8),
                    in_=shupw[:, iq * 512:(iq + 1) * 512]
                    .rearrange("(c k) m -> k c m", c=8)),
                   writes=[bSHUP[sl]], dma=True)
                op("gp", lambda g, iq=iq, sl=sl: g.dma_start(
                    out=SHDN[sl].rearrange("p (c m) -> p c m", c=4),
                    in_=shdnw[iq * 512:(iq + 1) * 512, :]
                    .rearrange("(c k) m -> k c m", c=4)),
                   writes=[bSHDN[sl]], dma=True)
                for i2c in range(4):
                    for tcb in range(4):
                        b = (i2c * 4 + tcb) % 4
                        for c in range(8):
                            op("pe", lambda t, sl=sl, i2c=i2c, tcb=tcb,
                               c=c, b=b: t.matmul(
                                   bk(b),
                                   SHUP[sl][:, c * 512 + i2c * 128:
                                            c * 512 + i2c * 128 + 128],
                                   H1T[:, c * 2048 + tcb * 512:
                                       c * 2048 + tcb * 512 + 512],
                                   start=(c == 0), stop=(c == 7)),
                               reads=[bSHUP[sl], bH1T], writes=[psb[b]])
                        op("ae", lambda a, i2c=i2c, tcb=tcb, b=b:
                           a.activation(
                               SHS[:, i2c * 2048 + tcb * 512:
                                   i2c * 2048 + tcb * 512 + 512],
                               bk(b), AF.Silu),
                           reads=[psb[b]], writes=[bSHS])
                for tt in range(NT):
                    b0 = 4 if tt % 2 == 0 else 6
                    for i2c in range(4):
                        for hf in range(2):
                            op("pe", lambda t, sl=sl, i2c=i2c, tt=tt, hf=hf,
                               b0=b0: t.matmul(
                                   bk(b0 + hf),
                                   SHS[:, i2c * 2048 + tt * 128:
                                       i2c * 2048 + tt * 128 + 128],
                                   SHDN[sl][:, i2c * 1024 + hf * 512:
                                            i2c * 1024 + hf * 512 + 512],
                                   start=(i2c == 0), stop=(i2c == 3)),
                               reads=[bSHS, bSHDN[sl]], writes=[psb[b0 + hf]])
                    op("ve", lambda v, tt=tt, b0=b0: v.tensor_add(
                        moe_t(tt), moe_t(tt), bk(b0, 2)),
                       reads=[psb[b0], psb[b0 + 1]], writes=[bMOE])

            tap("d_moe1", MOE, [bMOE], s == 0)

            WUP = [a16(24576, 2048), a16(26624, 2048)]
            bWUP = [Buf(reuse=[bSHUP[0]]), Buf(reuse=[bSHUP[1]])]
            WDN = [a16(28672, 2048), a16(30720, 2048)]
            bWDN = [Buf(reuse=[bSHDN[0]]), Buf(reuse=[bSHDN[1]])]
            SILU = a16(38912, 4096)
            bSILU = Buf(reuse=[bSHS])
            COMBE = [a16(33792, 1024), a16(34816, 1024)]
            bCOMBE = [Buf(reuse=[bSCT, bSCTOK, bH1B]),
                      Buf(reuse=[bSCT, bSCTOK, bH1B])]
            COMBB = [a16(43008, 1024), a16(44032, 1024)]
            bCOMBB = [Buf(reuse=[bLINK]), Buf(reuse=[bLINK])]

            for e in range(E):
                sl = e % 2
                op("gp", lambda g, e=e, sl=sl: g.dma_start(
                    out=WUP[sl].rearrange("p (c m) -> p c m", c=8),
                    in_=eup[e].rearrange("(c k) m -> k c m", c=8)),
                   writes=[bWUP[sl]], dma=True)
                op("gp", lambda g, e=e, sl=sl: g.dma_start(
                    out=WDN[sl].rearrange("p (c m) -> p c m", c=4),
                    in_=edn[e].rearrange("(c k) m -> k c m", c=4)),
                   writes=[bWDN[sl]], dma=True)
                op("sy", lambda s_, e=e, sl=sl: s_.dma_start(
                    out=COMBE[sl][0:1, :], in_=COMBT[0:16, :][e:e + 1, :]),
                   reads=[bCOMBT], writes=[bCOMBE[sl]], dma=True)
                for tcb in range(4):
                    b = tcb % 2
                    op("pe", lambda t, sl=sl, tcb=tcb, b=b: t.matmul(
                        bk(b), ONESB, COMBE[sl][0:1, tcb * 512:(tcb + 1) * 512],
                        start=True, stop=True),
                       reads=[bONESB, bCOMBE[sl]], writes=[psb[b]])
                    op("ve", lambda v, sl=sl, tcb=tcb, b=b: v.tensor_copy(
                        COMBB[sl][:, tcb * 512:(tcb + 1) * 512], bk(b)),
                       reads=[psb[b]], writes=[bCOMBB[sl]])
                for ic in range(4):
                    for tcb in range(4):
                        b = (ic * 4 + tcb) % 4
                        for c in range(8):
                            op("pe", lambda t, sl=sl, ic=ic, tcb=tcb, c=c,
                               b=b: t.matmul(
                                   bk(b),
                                   WUP[sl][:, c * 512 + ic * 128:
                                           c * 512 + ic * 128 + 128],
                                   H1T[:, c * 2048 + tcb * 512:
                                       c * 2048 + tcb * 512 + 512],
                                   start=(c == 0), stop=(c == 7)),
                               reads=[bWUP[sl], bH1T], writes=[psb[b]])
                        op("ae", lambda a, ic=ic, tcb=tcb, b=b: a.activation(
                            SILU[:, ic * 2048 + tcb * 512:
                                 ic * 2048 + tcb * 512 + 512],
                            bk(b), AF.Silu),
                           reads=[psb[b]], writes=[bSILU])
                    op("ve", lambda v, sl=sl, ic=ic: v.tensor_mul(
                        SILU[:, ic * 2048:(ic + 1) * 2048],
                        SILU[:, ic * 2048:(ic + 1) * 2048],
                        COMBB[sl]),
                       reads=[bCOMBB[sl]], writes=[bSILU])
                for tt in range(NT):
                    b0 = 4 if tt % 2 == 0 else 6
                    for ic in range(4):
                        for hf in range(2):
                            op("pe", lambda t, sl=sl, ic=ic, tt=tt, hf=hf,
                               b0=b0: t.matmul(
                                   bk(b0 + hf),
                                   SILU[:, ic * 2048 + tt * 128:
                                        ic * 2048 + tt * 128 + 128],
                                   WDN[sl][:, ic * 1024 + hf * 512:
                                           ic * 1024 + hf * 512 + 512],
                                   start=(ic == 0), stop=(ic == 3)),
                               reads=[bSILU, bWDN[sl]], writes=[psb[b0 + hf]])
                    op("ve", lambda v, tt=tt, b0=b0: v.tensor_add(
                        moe_t(tt), moe_t(tt), bk(b0, 2)),
                       reads=[psb[b0], psb[b0 + 1]], writes=[bMOE])

            OUTB = [a32(0, 1024), a32(1024, 1024)]
            bOUT = [Buf(reuse=[bH1T]), Buf(reuse=[bH1T])]
            for tt in range(NT):
                sl = tt % 2
                op("ve", lambda v, tt=tt: v.bn_stats(
                    ST6[:, 0:6], moe_t(tt)[:, 0:512]),
                   reads=[bMOE], writes=[bST6])
                op("ve", lambda v, tt=tt: v.bn_stats(
                    ST6[:, 6:12], moe_t(tt)[:, 512:1024]),
                   reads=[bMOE], writes=[bST6])
                op("ve", lambda v: v.bn_aggr(MV[:, 0:2], ST6),
                   reads=[bST6], writes=[bMVb])
                op("ae", lambda a: a.activation(RT1[:, 0:1], MV[:, 1:2],
                                                AF.Sqrt, bias=EPSB),
                   reads=[bMVb, bEPSB], writes=[bRT1])
                op("ve", lambda v: v.reciprocal(RT1[:, 1:2], RT1[:, 0:1]),
                   writes=[bRT1])
                op("ve", lambda v: v.scalar_tensor_tensor(
                    out=RT1[:, 2:3], in0=MV[:, 0:1], scalar=-1.0,
                    in1=RT1[:, 1:2], op0=AL.mult, op1=AL.mult),
                   reads=[bMVb], writes=[bRT1])
                op("ae", lambda a, tt=tt, sl=sl: a.activation(
                    OUTB[sl], moe_t(tt), AF.Identity, bias=RT1[:, 2:3],
                    scale=RT1[:, 1:2]),
                   reads=[bRT1, bMOE], writes=[bOUT[sl]])
                op("ae", lambda a, tt=tt, sl=sl, ys=ys: a.dma_start(
                    out=ys[tt * 128:(tt + 1) * 128, :], in_=OUTB[sl]),
                   reads=[bOUT[sl]], dma=True)
                nstore[0] += 1

            K.barrier()

        dae = K.dsem["ae"]
        K.prog["ae"].append(([(dae, dae.val)], None, None, 0))
        dgp = K.dsem["gp"]
        K.prog["gp"].append(([(dgp, dgp.val)], None, None, 0))
        dsy = K.dsem["sy"]
        K.prog["sy"].append(([(dsy, dsy.val)], None, None, 0))

        with nc.Block() as block:
            @block.tensor
            def _(t):
                K.run({"pe": t})

            @block.vector
            def _(v):
                K.run({"ve": v})

            @block.scalar
            def _(a):
                K.run({"ae": a})

            @block.gpsimd
            def _(g):
                K.run({"gp": g})

            @block.sync
            def _(sy):
                K.run({"sy": sy})

    return nc


_NC_CACHE = [None]


def kernel(L0_attn_x, L0_attn_Wqkv, L0_attn_bqkv, L0_attn_W1p, L0_attn_W2p,
           L0_attn_plucker_scale, L0_attn_Wout, L0_attn_bout,
           L0_ffn_gate_weight, L0_ffn_expert_up, L0_ffn_expert_down,
           L0_ffn_shared_up, L0_ffn_shared_down):
    x = np.ascontiguousarray(np.asarray(L0_attn_x, dtype=np.float32))
    B, Tq, Dm = x.shape
    assert (B, Tq, Dm) == (16, T, D)
    w = {
        "wqkv": np.ascontiguousarray(np.asarray(L0_attn_Wqkv, np.float32)),
        "bqkv": np.ascontiguousarray(np.asarray(L0_attn_bqkv, np.float32)),
        "w1p": np.ascontiguousarray(np.asarray(L0_attn_W1p, np.float32)),
        "w2p": np.ascontiguousarray(np.asarray(L0_attn_W2p, np.float32)),
        "pscale": np.ascontiguousarray(
            np.asarray(L0_attn_plucker_scale, np.float32)),
        "wout": np.ascontiguousarray(np.asarray(L0_attn_Wout, np.float32)),
        "bout": np.ascontiguousarray(np.asarray(L0_attn_bout, np.float32)),
        "gate": np.ascontiguousarray(
            np.asarray(L0_ffn_gate_weight, np.float32)),
        "eup": np.ascontiguousarray(np.asarray(L0_ffn_expert_up, np.float32)),
        "edn": np.ascontiguousarray(
            np.asarray(L0_ffn_expert_down, np.float32)),
        "shup": np.ascontiguousarray(
            np.asarray(L0_ffn_shared_up, np.float32)),
        "shdn": np.ascontiguousarray(
            np.asarray(L0_ffn_shared_down, np.float32)),
    }
    if _NC_CACHE[0] is None:
        _NC_CACHE[0] = build_program()
    nc = _NC_CACHE[0]
    in_maps = []
    for c in range(8):
        m = {"x": np.ascontiguousarray(
            x[2 * c:2 * c + 2].reshape(NSEQ * T, D))}
        m.update(w)
        in_maps.append(m)
    res = run_bass_kernel_spmd(nc, in_maps, core_ids=list(range(8)))
    out = np.concatenate(
        [r["y"].reshape(NSEQ, T, D) for r in res.results], axis=0)
    return out.astype(np.float32)
